# version 08
# baseline (speedup 1.0000x reference)
import math
import os
import numpy as np
from contextlib import ExitStack

import concourse.bass as bass
import concourse.mybir as mybir
from concourse.bass_utils import run_bass_kernel_spmd

F32 = mybir.dt.float32
BF16 = mybir.dt.bfloat16
I32 = mybir.dt.int32
AF = mybir.ActivationFunctionType
OP = mybir.AluOpType

D = 1024
FF = 4096
H = 16
LB = 16384
OWN = 4096
NT_P1 = OWN // 128
SUP = 512
NSUP = OWN // SUP
CRND = 16384.0
LN_EPS = 1e-5
DEN_EPS = 1e-6
KDBG = int(os.environ.get("KDBG", "0"))
KPH = int(os.environ.get("KPH", "2"))
KP2 = int(os.environ.get("KP2", "9"))
KP3 = int(os.environ.get("KP3", "9"))


def _pwl_consts():
    def gelu(x):
        return 0.5 * x * (1.0 + math.erf(x / math.sqrt(2.0)))
    Gy = [gelu(j * 0.25) - 0.5 * (j * 0.25) for j in range(17)]
    s = [Gy[j + 1] - Gy[j] for j in range(16)]
    return (256.0 * s[0],
            [256.0 * (s[j] - s[j - 1]) for j in range(1, 16)],
            32.0 - 256.0 * s[15],
            512.0 - 256.0 * Gy[16])


ENGS = ("sync", "gpsimd", "vector", "scalar", "tensor")


SEMKEYS = [e for e in ENGS] + [e + "d" for e in ENGS]
SELF_WAIT = {"vector", "scalar", "gpsimd"}


class Sched:
    def __init__(self):
        self.ops = {e: [] for e in ENGS}
        self.cnt = {k: 0 for k in SEMKEYS}

    def emit(self, eng, fn, waits=(), inc=1, key=None):
        key = key or eng
        before = self.cnt[key]
        self.ops[eng].append((fn, tuple(w for w in waits if w is not None), inc, key,
                              before))
        if inc:
            self.cnt[key] += inc
        return (key, self.cnt[key])

    def dma(self, eng, out, in_, waits=(), slow=False):
        if slow:
            return self.emit(eng, lambda e, o=out, i=in_: e.dma_start(
                out=o, in_=i, allow_slow_non_contiguous=True), waits=waits, inc=16,
                key=eng + "d")
        return self.emit(eng, lambda e, o=out, i=in_: e.dma_start(out=o, in_=i),
                         waits=waits, inc=16, key=eng + "d")

    def token(self, eng):
        return (eng, self.cnt[eng])

    def dtoken(self, eng):
        return (eng + "d", self.cnt[eng + "d"])

    def barrier(self):
        toks = [self.token(e) for e in ENGS] + [self.dtoken(e) for e in ENGS]
        for e in ENGS:
            waits = [t for t in toks if t[0] != e]
            self.emit(e, lambda eng: eng.nop(), waits=waits, inc=1)

    def replay(self, name, eng, sems):
        last = {}
        for fn, waits, inc, key, before in self.ops[name]:
            allw = list(waits)
            if name in SELF_WAIT and key == name and before > 0:
                allw.append((name, before))
            for (pkey, val) in allw:
                if last.get(pkey, 0) >= val:
                    continue
                eng.wait_ge(sems[pkey], val)
                last[pkey] = val
            ins = fn(eng)
            if inc:
                ins.then_inc(sems[key], inc)


def build_nc():
    c_lin, dels, a_e, b_e = _pwl_consts()
    nc = bass.Bass("TRN2", target_bir_lowering=False)

    x_own = nc.dram_tensor("x_own", [OWN, D], F32, kind="ExternalInput").ap()
    c_in = nc.dram_tensor("c_in", [1, D], F32, kind="ExternalInput").ap()
    wmod = nc.dram_tensor("wmod", [D, 6 * D], F32, kind="ExternalInput").ap()
    gamma = nc.dram_tensor("gamma", [D], F32, kind="ExternalInput").ap()
    beta = nc.dram_tensor("beta", [D], F32, kind="ExternalInput").ap()
    wqkv = nc.dram_tensor("wqkv", [D, 3 * D], F32, kind="ExternalInput").ap()
    wout = nc.dram_tensor("wout", [D, D], F32, kind="ExternalInput").ap()
    w1 = nc.dram_tensor("w1", [D, FF], F32, kind="ExternalInput").ap()
    b1 = nc.dram_tensor("b1", [FF], F32, kind="ExternalInput").ap()
    w2 = nc.dram_tensor("w2", [FF, D], F32, kind="ExternalInput").ap()
    b2 = nc.dram_tensor("b2", [D], F32, kind="ExternalInput").ap()
    out = nc.dram_tensor("out", [OWN, D], F32, kind="ExternalOutput").ap()
    if KDBG:
        dbg_mod = nc.dram_tensor("dbg_mod", [1, 6 * D], F32, kind="ExternalOutput").ap()
        dbg_bw = nc.dram_tensor("dbg_bw", [1, 3 * D], F32, kind="ExternalOutput").ap()
        dbg_kvks = nc.dram_tensor("dbg_kvks", [128, 1040], F32, kind="ExternalOutput").ap()
        dbg_hn = nc.dram_tensor("dbg_hn", [128, D], F32, kind="ExternalOutput").ap()
        dbg_q = nc.dram_tensor("dbg_q", [128, SUP], F32, kind="ExternalOutput").ap()
        dbg_att = nc.dram_tensor("dbg_att", [128, D], F32, kind="ExternalOutput").ap()
        dbg_x2 = nc.dram_tensor("dbg_x2", [128, D], F32, kind="ExternalOutput").ap()
        dbg_yq = nc.dram_tensor("dbg_yq", [128, SUP], F32, kind="ExternalOutput").ap()
    w1c = nc.dram_tensor("w1c", [FF // 128, 128, 1024], BF16).ap()
    w2c = nc.dram_tensor("w2c", [D // 128, 128, FF], BF16).ap()
    rowscr = nc.dram_tensor("rowscr", [1, D], F32).ap()
    cc_in = nc.dram_tensor("cc_in", [128, 1040], F32, kind="Internal",
                           addr_space="Local").ap() if False else \
        nc.dram_tensor("cc_in", [128, 1040], F32).ap()
    cc_out = nc.dram_tensor("cc_out", [128, 1040], F32).ap()

    S = Sched()

    with ExitStack() as st:
        e = st.enter_context
        ident = e(nc.sbuf_tensor("ident", [128, 128], BF16))
        wq16 = [e(nc.sbuf_tensor(f"wq16_{ch}", [128, D], BF16)) for ch in range(8)]
        wo16 = [e(nc.sbuf_tensor(f"wo16_{ch}", [128, D], BF16)) for ch in range(8)]
        bc_1ca = e(nc.sbuf_tensor("bc_1ca", [128, D], F32))
        bc_sa = e(nc.sbuf_tensor("bc_sa", [128, D], F32))
        bc_ga = e(nc.sbuf_tensor("bc_ga", [128, D], F32))
        bc_1cb = e(nc.sbuf_tensor("bc_1cb", [128, D], F32))
        bc_sb = e(nc.sbuf_tensor("bc_sb", [128, D], F32))
        bc_gb = e(nc.sbuf_tensor("bc_gb", [128, D], F32))
        bc_bw = e(nc.sbuf_tensor("bc_bw", [128, 2 * D], F32))
        bwq = e(nc.sbuf_tensor("bwq", [128, 8], F32))
        b1c = e(nc.sbuf_tensor("b1c", [128, 32], F32))
        b2c = e(nc.sbuf_tensor("b2c", [128, 8], F32))
        gfm = e(nc.sbuf_tensor("gfm", [128, 8], F32))
        kvks = e(nc.sbuf_tensor("kvks", [128, 16 * 65], BF16))
        scfm = e(nc.sbuf_tensor("scfm", [128, 8], F32))
        befm = e(nc.sbuf_tensor("befm", [128, 8], BF16))
        ioti = e(nc.sbuf_tensor("ioti", [128, 128], I32))

        PS_A = e(nc.psum_tensor("PS_A", [128, 2048], F32))
        PS_B = e(nc.psum_tensor("PS_B", [128, 1024], F32))
        PS_C = e(nc.psum_tensor("PS_C", [128, 512], F32))
        PS_T = e(nc.psum_tensor("PS_T", [128, 1024], BF16))

        if True:
            stA = ExitStack()
            eA = stA.enter_context
            wkv16 = [eA(nc.sbuf_tensor(f"wkv16_{ch}", [128, 2 * D], BF16)) for ch in range(8)]
            st0 = ExitStack()
            e0 = st0.enter_context
            wf = e0(nc.sbuf_tensor("wf", [128, 3 * D], F32))
            wmb = e0(nc.sbuf_tensor("wmb", [128, 8 * 512], F32))
            w1f = [e0(nc.sbuf_tensor(f"w1f{i}", [128, FF], BF16)) for i in range(2)]
            w2f = [e0(nc.sbuf_tensor(f"w2f{i}", [128, D], BF16)) for i in range(2)]
            cro = e0(nc.sbuf_tensor("cro", [128, 8], F32))
            modrow = e0(nc.sbuf_tensor("modrow", [1, 6 * D], F32))
            bwrow = e0(nc.sbuf_tensor("bwrow", [1, 3 * D], F32))
            sgnscr = e0(nc.sbuf_tensor("sgnscr", [1, D], F32))

            ti = S.emit("gpsimd", lambda g: g.iota(ioti[:], pattern=[[-1, 128]], base=0,
                                                   channel_multiplier=1))
            tid = S.emit("vector", lambda v: v.tensor_single_scalar(
                ident[:], ioti[:], 0.0, OP.is_equal), waits=[ti])

            tg = S.dma("sync", gfm[:], gamma.rearrange("(ch p) -> p ch", p=128), slow=True)
            tc = S.dma("sync", cro[:], c_in.rearrange("a (ch p) -> p (a ch)", p=128), slow=True)
            tbe = S.dma("gpsimd", befm[:], beta.rearrange("(ch p) -> p ch", p=128), slow=True)
            tb1 = S.dma("sync", b1c[:], b1.rearrange("(c p) -> p c", p=128), slow=True)
            tb2 = S.dma("sync", b2c[:], b2.rearrange("(c p) -> p c", p=128), slow=True)
            S.emit("vector", lambda v: v.tensor_scalar(b1c[:], b1c[:], 256.0, CRND + 0.5,
                                                       OP.mult, OP.add), waits=[tb1])
            S.emit("vector", lambda v: v.tensor_scalar(b2c[:], b2c[:], 256.0, CRND + 0.5,
                                                       OP.mult, OP.add), waits=[tb2])
            tsc = S.emit("scalar", lambda s: s.activation(scfm[:], cro[:], AF.Silu), waits=[tc])

            wun = e0(nc.sbuf_tensor("wun", [128, 3 * D], BF16))
            bw_ps = [PS_A[0:1, ct * 512:(ct + 1) * 512] for ct in range(4)] + \
                [PS_B[0:1, 0:512], PS_C[0:1, 0:512]]
            tprev = None
            for ch in range(8):
                td = S.dma("sync", wf[:], wqkv[ch * 128:(ch + 1) * 128, :],
                           waits=[tprev] if tprev else [])
                t1 = S.emit("vector", lambda v, ch=ch: v.tensor_scalar(
                    wq16[ch][:], wf[:, 0:D], gfm[:, ch:ch + 1], None, OP.mult),
                    waits=[td, tg])
                t2 = S.emit("vector", lambda v, ch=ch: v.tensor_scalar(
                    wkv16[ch][:], wf[:, D:3 * D], gfm[:, ch:ch + 1], None, OP.mult),
                    waits=[td])
                tun = S.emit("vector", lambda v: v.tensor_copy(wun[:], wf[:]), waits=[td])
                tprev = tun
                for ct in range(6):
                    S.emit("tensor", lambda t, ch=ch, ct=ct: (
                        t.ldweights(befm[:, ch:ch + 1]),
                        t.matmul(bw_ps[ct], befm[:, ch:ch + 1],
                                 wun[:, ct * 512:(ct + 1) * 512],
                                 start=(ch == 0), stop=(ch == 7)))[-1],
                        waits=([tun, tbe] if ct == 0 else ()), inc=1)
            wcast_done = S.token("vector")
            tbwmm = S.token("tensor")
            pev = None
            for ct in range(6):
                pev = S.emit("vector", lambda v, ct=ct: v.tensor_copy(
                    bwrow[0:1, ct * 512:(ct + 1) * 512], bw_ps[ct]), waits=[tbwmm])
            tbw = pev
            tbwq = S.dma("gpsimd", rowscr[0:1, :], bwrow[0:1, 0:D], waits=[tbw])
            tbwq = S.dma("gpsimd", bwq[:],
                         rowscr.rearrange("a (ch p) -> p (a ch)", p=128), waits=[tbwq],
                         slow=True)

            for ct in range(12):
                dts = None
                for ch in range(8):
                    dts = S.dma("sync", wmb[:, ch * 512:(ch + 1) * 512],
                                wmod[ch * 128:(ch + 1) * 128, ct * 512:(ct + 1) * 512],
                                waits=[pev] if ch == 0 else [])
                for ch in range(8):
                    S.emit("tensor", lambda t, ch=ch: t.matmul(
                        PS_C[0:1, :], scfm[:, ch:ch + 1],
                        wmb[:, ch * 512:(ch + 1) * 512],
                        start=(ch == 0), stop=(ch == 7)),
                        waits=([dts, tsc, pev] if ch == 0 else ()), inc=1)
                pev = S.emit("vector", lambda v, ct=ct: v.tensor_copy(
                    modrow[0:1, ct * 512:(ct + 1) * 512], PS_C[0:1, :]),
                    waits=[S.token("tensor")])
            tmod = pev

            S.emit("vector", lambda v: v.tensor_single_scalar(
                modrow[0:1, D:2 * D], modrow[0:1, D:2 * D], 1.0, OP.add), waits=[tmod])
            S.emit("vector", lambda v: v.tensor_single_scalar(
                modrow[0:1, 4 * D:5 * D], modrow[0:1, 4 * D:5 * D], 1.0, OP.add))
            for ofs in (2 * D, 5 * D):
                S.emit("vector", lambda v, o=ofs: v.tensor_single_scalar(
                    sgnscr[0:1, :], modrow[0:1, o:o + D], 0.0, OP.is_ge))
                S.emit("vector", lambda v: v.tensor_scalar(
                    sgnscr[0:1, :], sgnscr[0:1, :], 2.0, -1.0, OP.mult, OP.add))
                S.emit("scalar", lambda s_, o=ofs: s_.activation(
                    modrow[0:1, o:o + D], modrow[0:1, o:o + D], AF.Abs))
                S.emit("vector", lambda v, o=ofs: v.tensor_single_scalar(
                    modrow[0:1, o:o + D], modrow[0:1, o:o + D], 0.25, OP.max),
                    waits=[S.token("scalar")])
                S.emit("vector", lambda v, o=ofs: v.tensor_tensor(
                    modrow[0:1, o:o + D], modrow[0:1, o:o + D], sgnscr[0:1, :], OP.mult))
            tmod2 = S.token("vector")

            def bcast(dst, src_row, wtok, width):
                t = S.dma("sync", dst[0:1, 0:width], src_row, waits=[wtok])
                k = 1
                while k < 128:
                    t = S.dma("sync", dst[k:2 * k, 0:width], dst[0:k, 0:width], waits=[t])
                    k *= 2
                return t

            if KDBG:
                S.dma("sync", dbg_mod[0:1, :], modrow[0:1, :], waits=[tmod2])
                S.dma("sync", dbg_bw[0:1, :], bwrow[0:1, :], waits=[tbw])
            tb_sa = bcast(bc_sa, modrow[0:1, 0:D], tmod2, D)
            tb_ca = bcast(bc_1ca, modrow[0:1, D:2 * D], tmod2, D)
            tb_ga = bcast(bc_ga, modrow[0:1, 2 * D:3 * D], tmod2, D)
            tb_sb = bcast(bc_sb, modrow[0:1, 3 * D:4 * D], tmod2, D)
            tb_cb = bcast(bc_1cb, modrow[0:1, 4 * D:5 * D], tmod2, D)
            tb_gb = bcast(bc_gb, modrow[0:1, 5 * D:6 * D], tmod2, D)
            tb_bw = bcast(bc_bw, bwrow[0:1, D:3 * D], tbw, 2 * D)

            for ch in range(8):
                S.dma("gpsimd", wo16[ch][:], wout[ch * 128:(ch + 1) * 128, :])
            two16 = S.token("gpsimd")

            outs_done = {}
            for ch in range(8):
                bi = ch % 2
                tin = S.dma("gpsimd", w1f[bi][:], w1[ch * 128:(ch + 1) * 128, :],
                            waits=[outs_done.get(bi)])
                t = None
                for grp in range(4):
                    src = w1f[bi][:, grp * 1024:(grp + 1) * 1024]
                    src = src.rearrange("p (c j) -> p c j", j=128)
                    dst = w1c[grp * 8:(grp + 1) * 8, :, ch * 128:(ch + 1) * 128]
                    t = S.dma("gpsimd", dst.transpose([1, 0, 2]), src, waits=[tin])
                outs_done[bi] = t
            w1c_done = S.token("gpsimd")
            for ch in range(FF // 128):
                bi = ch % 2
                tin = S.dma("gpsimd", w2f[bi][:], w2[ch * 128:(ch + 1) * 128, :],
                            waits=[outs_done.get(8 + bi)])
                src = w2f[bi][:].rearrange("p (oc j) -> p oc j", j=128)
                dst = w2c[:, :, ch * 128:(ch + 1) * 128]
                t = S.dma("gpsimd", dst.transpose([1, 0, 2]), src, waits=[tin])
                outs_done[8 + bi] = t
            w2c_done = S.token("gpsimd")

            S.barrier()
            st0.close()

            with ExitStack() as st1:
                e1 = st1.enter_context
                xt = [e1(nc.sbuf_tensor(f"xt{i}", [128, D], F32)) for i in range(2)]
                sdump = e1(nc.sbuf_tensor("sdump", [128, D], F32))
                xnb = e1(nc.sbuf_tensor("xnb", [128, D], F32))
                h1b = e1(nc.sbuf_tensor("h1b", [128, D], F32))
                hn16 = e1(nc.sbuf_tensor("hn16", [128, D], BF16))
                hnT = e1(nc.sbuf_tensor("hnT", [128, D], BF16))
                kb = e1(nc.sbuf_tensor("kb", [128, D], F32))
                mt = e1(nc.sbuf_tensor("mt", [128, D], F32))
                et = e1(nc.sbuf_tensor("et", [128, D], BF16))
                rt = e1(nc.sbuf_tensor("rt", [128, D], BF16))
                k16 = e1(nc.sbuf_tensor("k16", [128, D], BF16))
                vext = e1(nc.sbuf_tensor("vext", [128, 16 * 65], BF16))
                stq = e1(nc.sbuf_tensor("stq", [128, 8], F32))

                for hh in range(16):
                    S.emit("vector", lambda v, hh=hh: v.memset(
                        vext[:, hh * 65 + 64:hh * 65 + 65], 1.0))
                vext_ones = S.token("vector")

                stq_free = [None]

                def ln_stats(src, wtok, sqb, mu_sl, rstd_sl):
                    t1 = S.emit("scalar", lambda s: s.activation(
                        sdump[:], src, AF.Identity, accum_out=sqb[:, 0:1]),
                        waits=[wtok, stq_free[0]])
                    t2 = S.emit("scalar", lambda s: s.activation(
                        sdump[:], src, AF.Square, accum_out=sqb[:, 1:2]), waits=[wtok])
                    a1 = S.emit("vector", lambda v: v.tensor_single_scalar(
                        mu_sl, sqb[:, 0:1], 1.0 / D, OP.mult), waits=[t1])
                    a2 = S.emit("vector", lambda v: v.tensor_scalar(
                        sqb[:, 3:4], sqb[:, 1:2], 1.0 / D, LN_EPS, OP.mult, OP.add), waits=[t2])
                    stq_free[0] = a2
                    a3 = S.emit("vector", lambda v: v.tensor_tensor(
                        sqb[:, 4:5], mu_sl, mu_sl, OP.mult), waits=[a1])
                    a4 = S.emit("vector", lambda v: v.tensor_tensor(
                        sqb[:, 5:6], sqb[:, 3:4], sqb[:, 4:5], OP.subtract), waits=[a2, a3])
                    a5 = S.emit("scalar", lambda s: s.activation(
                        sqb[:, 6:7], sqb[:, 5:6], AF.Ln), waits=[a4])
                    return S.emit("scalar", lambda s: s.activation(
                        rstd_sl, sqb[:, 6:7], AF.Exp, scale=-0.5), waits=[a5])

                xprev = {}
                for i in range(NT_P1 if KPH >= 1 else 0):
                    xb = xt[i % 2]
                    tld = S.dma("sync", xb[:], x_own[i * 128:(i + 1) * 128, :],
                                waits=[xprev.get(i % 2)])
                    ts1 = ln_stats(xb[:], tld, stq, stq[:, 2:3], stq[:, 7:8])
                    t = S.emit("vector", lambda v, xb=xb: v.tensor_scalar(
                        xnb[:], xb[:], stq[:, 2:3], stq[:, 7:8], OP.subtract, OP.mult),
                        waits=[ts1])
                    xprev[i % 2] = t
                    t = S.emit("vector", lambda v: v.tensor_tensor(
                        h1b[:], xnb[:], bc_1ca[:], OP.mult),
                        waits=[t] + ([tb_ca, tb_sa] if i == 0 else []))
                    th = S.emit("vector", lambda v: v.tensor_tensor(
                        h1b[:], h1b[:], bc_sa[:], OP.add), waits=[t])
                    ts2 = ln_stats(h1b[:], th, stq, stq[:, 2:3], stq[:, 7:8])
                    thn = S.emit("vector", lambda v: v.tensor_scalar(
                        hn16[:], h1b[:], stq[:, 2:3], stq[:, 7:8], OP.subtract, OP.mult),
                        waits=[ts2])
                    for ch in range(8):
                        S.emit("tensor", lambda t, ch=ch: t.transpose(
                            PS_T[:, ch * 128:(ch + 1) * 128], hn16[:, ch * 128:(ch + 1) * 128],
                            ident[:]), waits=[thn, tid] if ch == 0 else ())
                    tT = S.token("tensor")
                    tev = S.emit("vector", lambda v: v.tensor_copy(hnT[:], PS_T[:]), waits=[tT])
                    for ch in range(8):
                        for cc in range(4):
                            S.emit("tensor", lambda t, ch=ch, cc=cc: (
                                t.ldweights(hnT[:, ch * 128:(ch + 1) * 128]) if cc == 0 else None,
                                t.matmul(PS_A[:, cc * 512:(cc + 1) * 512],
                                         hnT[:, ch * 128:(ch + 1) * 128],
                                         wkv16[ch][:, cc * 512:(cc + 1) * 512],
                                         start=(ch == 0), stop=(ch == 7)))[-1],
                                waits=([tev, wcast_done] if (ch == 0 and cc == 0) else ()),
                                inc=1)
                    tkv = S.token("tensor")
                    t = S.emit("vector", lambda v: v.tensor_tensor(
                        kb[:], PS_A[:, 0:D], bc_bw[:, 0:D], OP.add),
                        waits=[tkv] + ([tb_bw] if i == 0 else []))
                    t1 = S.emit("vector", lambda v: v.tensor_single_scalar(
                        mt[:], kb[:], 0.0, OP.min), waits=[t])
                    t2 = S.emit("scalar", lambda s: s.activation(et[:], mt[:], AF.Exp),
                                waits=[t1])
                    t3 = S.emit("vector", lambda v: v.tensor_single_scalar(
                        rt[:], kb[:], 0.0, OP.max), waits=[t])
                    tk = S.emit("vector", lambda v: v.tensor_tensor(
                        k16[:], et[:], rt[:], OP.add), waits=[t2, t3])
                    tv = S.emit("vector", lambda v: v.tensor_tensor(
                        vext[:].rearrange("p (h n) -> p h n", h=16)[:, :, 0:64],
                        PS_A[:, D:2 * D].rearrange("p (h n) -> p h n", h=16),
                        bc_bw[:, D:2 * D].rearrange("p (h n) -> p h n", h=16), OP.add),
                        waits=[tkv] + ([vext_ones] if i == 0 else []))
                    for pr in range(8):
                        ps = PS_B[:, 0:260] if pr < 4 else PS_B[:, 512:772]
                        co = 65 * (pr % 4)
                        h0, h1_ = 2 * pr, 2 * pr + 1
                        S.emit("tensor", lambda t, ps=ps, co=co, h0=h0, i=i: (
                            t.ldweights(k16[:, h0 * 64:(h0 + 1) * 64], tile_position=(0, 0)),
                            t.matmul(ps[0:64, co:co + 65], k16[:, h0 * 64:(h0 + 1) * 64],
                                     vext[:, h0 * 65:(h0 + 1) * 65],
                                     start=(i == 0), stop=(i == NT_P1 - 1),
                                     tile_position=(0, 0)))[-1],
                            waits=([tk, tv] if pr == 0 else ()), inc=1)
                        S.emit("tensor", lambda t, ps=ps, co=co, h1_=h1_, i=i: (
                            t.ldweights(k16[:, h1_ * 64:(h1_ + 1) * 64], tile_position=(0, 64)),
                            t.matmul(ps[64:128, co:co + 65], k16[:, h1_ * 64:(h1_ + 1) * 64],
                                     vext[:, h1_ * 65:(h1_ + 1) * 65],
                                     start=(i == 0), stop=(i == NT_P1 - 1),
                                     tile_position=(0, 64)))[-1], inc=1)
                kv_mm_tok = S.token("tensor")
                kvf32 = e1(nc.sbuf_tensor("kvf32", [128, 1040], F32))
                if KPH >= 1:
                    zf = S.emit("vector", lambda v: v.memset(kvf32[:], 0.0))
                    for pr in range(8):
                        ps = PS_B[:, 0:260] if pr < 4 else PS_B[:, 512:772]
                        co = 65 * (pr % 4)
                        h0, h1_ = 2 * pr, 2 * pr + 1
                        S.emit("vector", lambda v, ps=ps, co=co, h0=h0: v.tensor_copy(
                            kvf32[0:64, h0 * 65:(h0 + 1) * 65], ps[0:64, co:co + 65]),
                            waits=[kv_mm_tok, zf])
                        S.emit("vector", lambda v, ps=ps, co=co, h1_=h1_: v.tensor_copy(
                            kvf32[64:128, h1_ * 65:(h1_ + 1) * 65], ps[64:128, co:co + 65]),
                            waits=[kv_mm_tok])
                    tkv_ev = S.token("vector")
                    tcc1 = S.dma("sync", cc_in[:, :], kvf32[:], waits=[tkv_ev])
                    tcol = S.emit("gpsimd", lambda g: g.collective_compute(
                        "AllReduce", OP.add,
                        replica_groups=[[0, 1, 2, 3], [4, 5, 6, 7]],
                        ins=[cc_in[:, :]], outs=[cc_out[:, :]]), waits=[tcc1])
                    tcc2 = S.dma("gpsimd", kvf32[:], cc_out[:, :], waits=[tcol])
                    S.emit("vector", lambda v: v.tensor_copy(kvks[:], kvf32[:]),
                           waits=[tcc2])

                if KDBG:
                    S.dma("gpsimd", dbg_kvks[:, :], kvks[:], waits=[S.token("vector")])
                S.barrier()

            stA.close()
            with ExitStack() as st2:
                e2 = st2.enter_context
                xt2 = [e2(nc.sbuf_tensor("xt2_0", [128, D], F32))]
                sdump2 = e2(nc.sbuf_tensor("sdump2", [128, D], BF16))
                hb2 = e2(nc.sbuf_tensor("hb2", [128, D], F32))
                hn2 = e2(nc.sbuf_tensor("hn2", [128, D], BF16))
                hnTs = e2(nc.sbuf_tensor("hnTs", [128, 8 * SUP], BF16))
                h2Ts = e2(nc.sbuf_tensor("h2Ts", [128, 8 * SUP], BF16))
                q16 = [e2(nc.sbuf_tensor(f"q16_{qc}", [128, SUP], BF16)) for qc in range(8)]
                mt2 = e2(nc.sbuf_tensor("mt2", [128, SUP], F32))
                et2 = e2(nc.sbuf_tensor("et2", [128, SUP], BF16))
                rt2 = e2(nc.sbuf_tensor("rt2", [128, SUP], BF16))
                att = e2(nc.sbuf_tensor("att", [128, D], BF16))
                attT = e2(nc.sbuf_tensor("attT", [128, D], BF16))
                oq16s = e2(nc.sbuf_tensor("oq16s", [128, 8 * SUP], BF16))
                rcd = e2(nc.sbuf_tensor("rcd", [128, 32], F32))
                x2s = e2(nc.sbuf_tensor("x2s", [128, 4 * D], BF16))
                st1v = e2(nc.sbuf_tensor("st1v", [128, 8], F32))
                stq2 = e2(nc.sbuf_tensor("stq2", [128, 8], F32))
                yq = e2(nc.sbuf_tensor("yq", [128, 32 * SUP], BF16))
                w1s = [e2(nc.sbuf_tensor(f"w1s_{i}", [128, 1024], BF16)) for i in range(2)]
                w2s = e2(nc.sbuf_tensor("w2s", [128, FF], BF16))
                W2 = 2 * SUP
                ib = [e2(nc.sbuf_tensor(f"ib{i}", [128, W2], I32)) for i in range(2)]
                fbf = [e2(nc.sbuf_tensor(f"fbf{i}", [128, W2], F32)) for i in range(2)]
                tbf = [e2(nc.sbuf_tensor(f"tbf{i}", [128, W2], BF16)) for i in range(2)]
                ubf = e2(nc.sbuf_tensor("ubf", [128, W2], BF16))
                acc = [e2(nc.sbuf_tensor(f"acc{i}", [128, W2], BF16)) for i in range(2)]
                abuf = [e2(nc.sbuf_tensor(f"abuf{i}", [128, W2], BF16)) for i in range(3)]
                btab = e2(nc.sbuf_tensor("btab", [128, 17], F32))
                x3b = [e2(nc.sbuf_tensor("x3b0", [128, D], F32))]

                stq2_free = [None]

                def ln_stats2(src, wtok, mu_sl, rstd_sl):
                    t1 = S.emit("scalar", lambda s: s.activation(
                        sdump2[:], src, AF.Identity, accum_out=stq2[:, 0:1]),
                        waits=[wtok, stq2_free[0]])
                    t2 = S.emit("scalar", lambda s: s.activation(
                        sdump2[:], src, AF.Square, accum_out=stq2[:, 1:2]), waits=[wtok])
                    a1 = S.emit("vector", lambda v: v.tensor_single_scalar(
                        mu_sl, stq2[:, 0:1], 1.0 / D, OP.mult), waits=[t1])
                    a2 = S.emit("vector", lambda v: v.tensor_scalar(
                        stq2[:, 3:4], stq2[:, 1:2], 1.0 / D, LN_EPS, OP.mult, OP.add),
                        waits=[t2])
                    stq2_free[0] = a2
                    a3 = S.emit("vector", lambda v: v.tensor_tensor(
                        stq2[:, 4:5], mu_sl, mu_sl, OP.mult), waits=[a1])
                    a4 = S.emit("vector", lambda v: v.tensor_tensor(
                        stq2[:, 5:6], stq2[:, 3:4], stq2[:, 4:5], OP.subtract), waits=[a2, a3])
                    a5 = S.emit("scalar", lambda s: s.activation(
                        stq2[:, 6:7], stq2[:, 5:6], AF.Ln), waits=[a4])
                    return S.emit("scalar", lambda s: s.activation(
                        rstd_sl, stq2[:, 6:7], AF.Exp, scale=-0.5), waits=[a5])

                _coefs = [c_lin] + dels + [a_e]
                for j in range(17):
                    S.emit("vector", lambda v, j=j: v.memset(
                        btab[:, j:j + 1], -abs(_coefs[j]) * float(j)))
                btab_done = S.token("vector")

                xprev = {}
                psA_free = [None, None]
                qmm_done = None
                w1mm_done = None
                w2mm_done = None
                oqT_done = None
                w2s_free = None
                zfree = {}
                qread = None
                for sp in range(NSUP if KPH >= 2 else 0):
                    for t4 in range(4):
                        gi = sp * 4 + t4
                        xb = xt2[0]
                        tld = S.dma("sync", xb[:], x_own[gi * 128:(gi + 1) * 128, :],
                                    waits=[xprev.get(0)])
                        tsA = ln_stats2(xb[:], tld, st1v[:, 2 * t4:2 * t4 + 1],
                                        st1v[:, 2 * t4 + 1:2 * t4 + 2])
                        t = S.emit("vector", lambda v, xb=xb, t4=t4: v.tensor_scalar(
                            hb2[:], xb[:], st1v[:, 2 * t4:2 * t4 + 1],
                            st1v[:, 2 * t4 + 1:2 * t4 + 2], OP.subtract, OP.mult),
                            waits=[tsA])
                        xprev[0] = t
                        t = S.emit("vector", lambda v: v.tensor_tensor(
                            hb2[:], hb2[:], bc_1ca[:], OP.mult),
                            waits=[t] + ([tb_ca, tb_sa, tb_ga, tb_sb, tb_cb, tb_gb]
                                         if sp == 0 and t4 == 0 else []))
                        th = S.emit("vector", lambda v: v.tensor_tensor(
                            hb2[:], hb2[:], bc_sa[:], OP.add), waits=[t])
                        tsB = ln_stats2(hb2[:], th, stq2[:, 2:3], stq2[:, 7:8])
                        thn = S.emit("vector", lambda v: v.tensor_scalar(
                            hn2[:], hb2[:], stq2[:, 2:3], stq2[:, 7:8],
                            OP.subtract, OP.mult), waits=[tsB])
                        if KDBG and sp == 0 and t4 == 0:
                            S.dma("gpsimd", dbg_hn[:, :], hn2[:], waits=[thn])
                            S.barrier()
                        for ch in range(8):
                            S.emit("tensor", lambda t, ch=ch: t.transpose(
                                PS_T[:, ch * 128:(ch + 1) * 128],
                                hn2[:, ch * 128:(ch + 1) * 128], ident[:]),
                                waits=[thn] if ch == 0 else ())
                        tT = S.token("tensor")
                        S.emit("vector", lambda v, t4=t4: v.tensor_copy(
                            hnTs[:].rearrange("p (ch n) -> p ch n", ch=8)
                            [:, :, t4 * 128:(t4 + 1) * 128],
                            PS_T[:].rearrange("p (ch n) -> p ch n", ch=8)),
                            waits=[tT] + ([qmm_done] if t4 == 0 else []))
                    thnT = S.token("vector")

                    for qc in range(8 if KP2 >= 2 else 0):
                        for ch in range(8):
                            S.emit("tensor", lambda t, qc=qc, ch=ch: (
                                t.ldweights(wq16[ch][:, qc * 128:(qc + 1) * 128]),
                                t.matmul(PS_C[:, :], wq16[ch][:, qc * 128:(qc + 1) * 128],
                                         hnTs[:, ch * SUP:(ch + 1) * SUP],
                                         start=(ch == 0), stop=(ch == 7)))[-1],
                                waits=([thnT, qread] if ch == 0 else ()), inc=1)
                        tq = S.token("tensor")
                        t1 = S.emit("vector", lambda v, qc=qc: v.tensor_scalar(
                            mt2[:], PS_C[:, :], bwq[:, qc:qc + 1], 0.0, OP.add, OP.min),
                            waits=[tq, tbwq])
                        t2 = S.emit("scalar", lambda s: s.activation(
                            et2[:], mt2[:], AF.Exp), waits=[t1])
                        qread = S.emit("vector", lambda v, qc=qc: v.tensor_scalar(
                            rt2[:], PS_C[:, :], bwq[:, qc:qc + 1], 0.0, OP.add, OP.max),
                            waits=[tq])
                        tq16 = S.emit("vector", lambda v, qc=qc: v.tensor_tensor(
                            q16[qc][:], et2[:], rt2[:], OP.add), waits=[t2])
                        if KDBG and sp == 0 and qc == 0:
                            S.dma("gpsimd", dbg_q[:, :], q16[0][:], waits=[tq16])
                            S.barrier()
                    tqall = S.token("vector")

                    for t4 in range(4 if KP2 >= 3 else 0):
                        gi = sp * 4 + t4
                        xb = xt2[0]
                        tld = S.dma("sync", xb[:], x_own[gi * 128:(gi + 1) * 128, :],
                                    waits=[xprev.get(0)])
                        for nh, hh in enumerate(list(range(0, 16, 2)) + list(range(1, 16, 2))):
                            qc, ro = hh // 2, (hh % 2) * 64
                            S.emit("tensor", lambda t, qc=qc, ro=ro, hh=hh, t4=t4: (
                                t.ldweights(q16[qc][ro:ro + 64, t4 * 128:(t4 + 1) * 128],
                                            tile_position=(ro, 0)),
                                t.matmul(PS_A[:, hh * 128:hh * 128 + 65],
                                         q16[qc][ro:ro + 64, t4 * 128:(t4 + 1) * 128],
                                         kvks[ro:ro + 64, hh * 65:(hh + 1) * 65],
                                         start=True, stop=True, tile_position=(ro, 0)))[-1],
                                waits=([tqall] + psA_free if nh == 0 and t4 == 0
                                       else ([tqall] if nh == 0 else ())), inc=1)
                        tat = S.token("tensor")
                        t1 = S.emit("vector", lambda v: v.tensor_single_scalar(
                            rcd[:, 0:16],
                            PS_A[:].rearrange("p (h n) -> p h n", h=16)[:, :, 64:65],
                            DEN_EPS, OP.add), waits=[tat])
                        t2 = S.emit("vector", lambda v: v.reciprocal(
                            rcd[:, 16:32], rcd[:, 0:16]), waits=[t1])
                        tat2 = S.emit("vector", lambda v: v.tensor_tensor(
                            att[:].rearrange("p (h n) -> p h n", h=16),
                            PS_A[:].rearrange("p (h n) -> p h n", h=16)[:, :, 0:64],
                            rcd[:, 16:32].unsqueeze(2).broadcast_to([128, 16, 64]), OP.mult),
                            waits=[t2])
                        if KP3 < 2:
                            continue
                        if KDBG and sp == 0 and t4 == 0:
                            S.dma("gpsimd", dbg_att[:, :], att[:], waits=[tat2])
                            S.barrier()
                        for ch in range(8):
                            S.emit("tensor", lambda t, ch=ch: t.transpose(
                                PS_T[:, ch * 128:(ch + 1) * 128],
                                att[:, ch * 128:(ch + 1) * 128], ident[:]),
                                waits=[tat2] if ch == 0 else ())
                        tT = S.token("tensor")
                        tev = S.emit("vector", lambda v: v.tensor_copy(attT[:], PS_T[:]),
                                     waits=[tT])
                        for ch in range(8):
                            for cc in range(2):
                                S.emit("tensor", lambda t, ch=ch, cc=cc: (
                                    t.ldweights(attT[:, ch * 128:(ch + 1) * 128])
                                    if cc == 0 else None,
                                    t.matmul(PS_B[:, cc * 512:(cc + 1) * 512],
                                             attT[:, ch * 128:(ch + 1) * 128],
                                             wo16[ch][:, cc * 512:(cc + 1) * 512],
                                             start=(ch == 0), stop=(ch == 7)))[-1],
                                    waits=([tev, two16] if (ch == 0 and cc == 0) else ()),
                                    inc=1)
                        two = S.token("tensor")
                        if KP3 < 3:
                            continue
                        t = S.emit("vector", lambda v, xb=xb, t4=t4: v.tensor_scalar(
                            hb2[:], xb[:], st1v[:, 2 * t4:2 * t4 + 1],
                            st1v[:, 2 * t4 + 1:2 * t4 + 2], OP.subtract, OP.mult),
                            waits=[tld])
                        t = S.emit("vector", lambda v: v.tensor_tensor(
                            hb2[:], hb2[:], bc_1ca[:], OP.mult), waits=[t])
                        t = S.emit("vector", lambda v: v.tensor_tensor(
                            hb2[:], hb2[:], bc_sa[:], OP.add), waits=[t])
                        t = S.emit("vector", lambda v: v.tensor_tensor(
                            hb2[:], hb2[:], PS_B[:], OP.add), waits=[t, two])
                        t = S.emit("vector", lambda v: v.tensor_tensor(
                            hb2[:], hb2[:], bc_ga[:], OP.mult), waits=[t])
                        tx2 = S.emit("vector", lambda v, xb=xb, t4=t4: v.tensor_tensor(
                            x2s[:, t4 * D:(t4 + 1) * D], hb2[:], xb[:], OP.add), waits=[t])
                        xprev[0] = tx2
                        if KDBG and sp == 0 and t4 == 0:
                            S.dma("sync", dbg_x2[:, :], x2s[:, 0:D], waits=[tx2])
                            S.barrier()
                        if KP3 < 4:
                            continue
                        tsB = ln_stats2(x2s[:, t4 * D:(t4 + 1) * D], tx2,
                                        stq2[:, 2:3], stq2[:, 7:8])
                        t = S.emit("vector", lambda v, t4=t4: v.tensor_scalar(
                            hb2[:], x2s[:, t4 * D:(t4 + 1) * D], stq2[:, 2:3], stq2[:, 7:8],
                            OP.subtract, OP.mult), waits=[tsB])
                        t = S.emit("vector", lambda v: v.tensor_tensor(
                            hb2[:], hb2[:], bc_1cb[:], OP.mult), waits=[t])
                        th2 = S.emit("vector", lambda v: v.tensor_tensor(
                            hn2[:], hb2[:], bc_sb[:], OP.add), waits=[t])
                        for ch in range(8):
                            S.emit("tensor", lambda t, ch=ch: t.transpose(
                                PS_T[:, ch * 128:(ch + 1) * 128],
                                hn2[:, ch * 128:(ch + 1) * 128], ident[:]),
                                waits=[th2] if ch == 0 else ())
                        tT = S.token("tensor")
                        S.emit("vector", lambda v, t4=t4: v.tensor_copy(
                            h2Ts[:].rearrange("p (ch n) -> p ch n", ch=8)
                            [:, :, t4 * 128:(t4 + 1) * 128],
                            PS_T[:].rearrange("p (ch n) -> p ch n", ch=8)),
                            waits=[tT] + ([w1mm_done] if t4 == 0 else []))
                    th2T = S.token("vector")

                    coefs = [c_lin] + dels + [a_e]
                    w1buf_free = {}
                    ring = {}
                    tbf_free = {}
                    preps = {}
                    yq_tok = {}

                    def emit_prep(p, sp):
                        c0 = 2 * p
                        pp = p % 2
                        nonlocal_toks = {}
                        for c in (c0, c0 + 1):
                            bi = c % 2
                            tw = S.dma("sync", w1s[bi][:], w1c[c, :, :],
                                       waits=[w1buf_free.get(bi)] + ([w1c_done] if sp == 0 and c < 2 else []))
                            zps = PS_C[:, :] if bi == 0 else PS_B[:, 0:512]
                            for ch in range(8):
                                S.emit("tensor", lambda t, bi=bi, ch=ch, zps=zps: (
                                    t.ldweights(w1s[bi][:, ch * 128:(ch + 1) * 128]),
                                    t.matmul(zps, w1s[bi][:, ch * 128:(ch + 1) * 128],
                                             h2Ts[:, ch * SUP:(ch + 1) * SUP],
                                             start=(ch == 0), stop=(ch == 7)))[-1],
                                    waits=([tw, th2T, zfree.get(bi), qread]
                                           if ch == 0 else ()), inc=1)
                            w1buf_free[bi] = S.token("tensor")
                            tz = S.token("tensor")
                            t = S.emit("vector", lambda v, zps=zps, c=c, bi=bi, pp=pp:
                                       v.tensor_scalar(
                                ib[pp][:, bi * SUP:(bi + 1) * SUP], zps, 256.0,
                                b1c[:, c:c + 1], OP.mult, OP.add),
                                waits=[tz] + ([w2mm_done] if c == 0 else []))
                            zfree[bi] = t
                        tf = S.emit("vector", lambda v, pp=pp: v.tensor_single_scalar(
                            fbf[pp][:], ib[pp][:], -CRND, OP.add), waits=[t])
                        ttb = S.emit("scalar", lambda s_, pp=pp: s_.activation(
                            tbf[pp][:], fbf[pp][:], AF.Abs, scale=1.0 / 64.0),
                            waits=[tf, tbf_free.get(pp)])
                        preps[p] = (tf, ttb)

                    def emit_chain(p, sp):
                        c0 = 2 * p
                        pp = p % 2
                        tf, ttb = preps.pop(p)
                        ta = S.emit("vector", lambda v, pp=pp: v.tensor_single_scalar(
                            acc[0][:], fbf[pp][:], 0.5, OP.mult), waits=[tf])
                        pa = 0
                        tr = None
                        for j in range(17):
                            co = coefs[j]
                            ab = abuf[j % 3]
                            tr = S.emit("scalar", lambda s_, j=j, co=co, ab=ab, pp=pp:
                                        s_.activation(
                                ab[:], tbf[pp][:], AF.Relu, scale=abs(co),
                                bias=btab[:, j:j + 1]),
                                waits=[ttb, ring.get(j % 3), btab_done])
                            ta = S.emit("vector", lambda v, pa=pa, ab=ab, co=co:
                                        v.tensor_tensor(
                                acc[1 - pa][:], acc[pa][:], ab[:],
                                OP.add if co > 0 else OP.subtract), waits=[tr, ta])
                            ring[j % 3] = ta
                            pa = 1 - pa
                        tbf_free[pp] = tr
                        tu = S.emit("vector", lambda v, pp=pp: v.tensor_single_scalar(
                            ubf[:], tbf[pp][:], 16.0, OP.is_gt), waits=[ttb])
                        ta = S.emit("vector", lambda v, pa=pa: v.scalar_tensor_tensor(
                            acc[1 - pa][:], ubf[:], b_e, acc[pa][:], OP.mult, OP.add),
                            waits=[tu, ta])
                        pa = 1 - pa
                        t = S.emit("vector", lambda v, pa=pa, pp=pp: v.tensor_single_scalar(
                            ib[pp][:], acc[pa][:], CRND + 0.5, OP.add), waits=[ta])
                        yq_tok[p] = S.emit("vector", lambda v, c0=c0, pp=pp: v.tensor_scalar(
                            yq[:, c0 * SUP:(c0 + 2) * SUP], ib[pp][:], -CRND, 1.0 / 256.0,
                            OP.add, OP.mult), waits=[t])

                    NPAIR = 16 if KP2 >= 4 else 0
                    for p in range(NPAIR):
                        emit_prep(p, sp)
                        if p > 0:
                            emit_chain(p - 1, sp)
                    if NPAIR:
                        emit_chain(NPAIR - 1, sp)
                    tyq = S.token("vector")

                    w2read = {}
                    for oc in range(8 if KP2 >= 5 else 0):
                        bi = oc % 2
                        zps = PS_A[:, bi * 512:(bi + 1) * 512]
                        tw = S.dma("sync", w2s[:], w2c[oc, :, :],
                                   waits=[w2s_free] + ([w2c_done] if sp == 0 and oc == 0 else []))
                        for ch in range(32):
                            S.emit("tensor", lambda t, ch=ch, zps=zps: (
                                t.ldweights(w2s[:, ch * 128:(ch + 1) * 128]),
                                t.matmul(zps, w2s[:, ch * 128:(ch + 1) * 128],
                                         yq[:, ch * SUP:(ch + 1) * SUP],
                                         start=(ch == 0), stop=(ch == 31)))[-1],
                                waits=[yq_tok.get(ch // 2)] +
                                      ([tw, w2read.get(bi)] if ch == 0 else []), inc=1)
                        w2s_free = S.token("tensor")
                        tz = S.token("tensor")
                        t = S.emit("vector", lambda v, zps=zps, oc=oc: v.tensor_scalar(
                            ib[0][:, 0:SUP], zps, 256.0, b2c[:, oc:oc + 1], OP.mult, OP.add),
                            waits=[tz] + ([oqT_done] if oc == 0 else []))
                        w2read[bi] = t
                        S.emit("vector", lambda v, oc=oc: v.tensor_scalar(
                            oq16s[:, oc * SUP:(oc + 1) * SUP], ib[0][:, 0:SUP], -CRND,
                            1.0 / 256.0, OP.add, OP.mult), waits=[t])
                    w2mm_done = w2s_free
                    psA_free = [w2read.get(0), w2read.get(1)]
                    qread = None
                    zfree = {0: zfree.get(0), 1: zfree.get(1)}
                    toqa = S.token("vector")

                    oqf_done = None
                    st_done = {}
                    for t4 in range(4 if KP2 >= 9 else 0):
                        for ch in range(8):
                            S.emit("tensor", lambda t, ch=ch, t4=t4: t.transpose(
                                PS_T[:, ch * 128:(ch + 1) * 128],
                                oq16s[:, ch * SUP + t4 * 128: ch * SUP + (t4 + 1) * 128],
                                ident[:]),
                                waits=([toqa, oqf_done] if ch == 0 else ()))
                        tT = S.token("tensor")
                        x3 = x3b[0]
                        oqf_done = S.emit("vector", lambda v, x3=x3: v.tensor_copy(
                            x3[:], PS_T[:]), waits=[tT, st_done.get(0)])
                        t = S.emit("vector", lambda v, x3=x3: v.tensor_tensor(
                            x3[:], x3[:], bc_gb[:], OP.mult), waits=[oqf_done])
                        tcv = S.emit("vector", lambda v, t4=t4: v.tensor_copy(
                            hb2[:], x2s[:, t4 * D:(t4 + 1) * D]), waits=[t])
                        tx3 = S.emit("vector", lambda v, x3=x3: v.tensor_tensor(
                            x3[:], x3[:], hb2[:], OP.add), waits=[tcv])
                        gi = sp * 4 + t4
                        st_done[0] = S.dma("sync", out[gi * 128:(gi + 1) * 128, :], x3[:],
                                           waits=[tx3])
                    oqT_done = S.token("tensor")
                    qmm_done = w2s_free
                    w1mm_done = w2s_free

            if KPH < 2 or KP2 < 9:
                with ExitStack() as std:
                    zb = std.enter_context(nc.sbuf_tensor("zb", [128, D], F32))
                    tz0 = S.emit("vector", lambda v: v.memset(zb[:], 0.0))
                    for gi in range(OWN // 128):
                        S.dma("sync", out[gi * 128:(gi + 1) * 128, :], zb[:], waits=[tz0])

        with ExitStack() as stb:
            eb = stb.enter_context
            sems = {k: eb(nc.semaphore(f"sem_{k}")) for k in SEMKEYS}
            block = eb(nc.Block())

            @block.sync
            def _(eng):
                S.replay("sync", eng, sems)
                eng.wait_ge(sems["syncd"], S.cnt["syncd"])

            @block.gpsimd
            def _(eng):
                S.replay("gpsimd", eng, sems)

            @block.vector
            def _(eng):
                S.replay("vector", eng, sems)

            @block.scalar
            def _(eng):
                S.replay("scalar", eng, sems)

            @block.tensor
            def _(eng):
                S.replay("tensor", eng, sems)

    return nc


_NC_CACHE = None


def kernel(L0_attn_x, L0_attn_gamma, L0_attn_beta, L0_attn_Wqkv, L0_attn_Wout,
           L0_ffn_w1, L0_ffn_b1, L0_ffn_w2, L0_ffn_b2, L0_c, L0_wmod):
    global _NC_CACHE
    x = np.ascontiguousarray(np.asarray(L0_attn_x, dtype=np.float32))
    if _NC_CACHE is None:
        _NC_CACHE = build_nc()
    nc = _NC_CACHE
    in_maps = []
    for core in range(8):
        b, q = core // 4, core % 4
        in_maps.append({
            "x_own": x[b, q * OWN:(q + 1) * OWN],
            "c_in": np.ascontiguousarray(np.asarray(L0_c, np.float32)[b:b + 1]),
            "wmod": np.asarray(L0_wmod, np.float32),
            "gamma": np.asarray(L0_attn_gamma, np.float32),
            "beta": np.asarray(L0_attn_beta, np.float32),
            "wqkv": np.asarray(L0_attn_Wqkv, np.float32),
            "wout": np.asarray(L0_attn_Wout, np.float32),
            "w1": np.asarray(L0_ffn_w1, np.float32),
            "b1": np.asarray(L0_ffn_b1, np.float32),
            "w2": np.asarray(L0_ffn_w2, np.float32),
            "b2": np.asarray(L0_ffn_b2, np.float32),
        })
    res = run_bass_kernel_spmd(nc, in_maps, core_ids=list(range(8)))
    outp = np.empty((2, LB, D), dtype=np.float32)
    for core in range(8):
        b, q = core // 4, core % 4
        outp[b, q * OWN:(q + 1) * OWN] = res.results[core]["out"]
    return outp


# version 10 (final)
# speedup vs baseline: 1.0091x; 1.0091x over previous
import math
import os
import numpy as np
from contextlib import ExitStack

import concourse.bass as bass
import concourse.mybir as mybir
from concourse.bass_utils import run_bass_kernel_spmd

F32 = mybir.dt.float32
BF16 = mybir.dt.bfloat16
I32 = mybir.dt.int32
AF = mybir.ActivationFunctionType
OP = mybir.AluOpType

D = 1024
FF = 4096
H = 16
LB = 16384
OWN = 4096
NT_P1 = OWN // 128
SUP = 512
NSUP = OWN // SUP
CRND = 16384.0
LN_EPS = 1e-5
DEN_EPS = 1e-6
KDBG = int(os.environ.get("KDBG", "0"))
KPH = int(os.environ.get("KPH", "2"))
KP2 = int(os.environ.get("KP2", "9"))
KP3 = int(os.environ.get("KP3", "9"))


def _pwl_consts():
    def gelu(x):
        return 0.5 * x * (1.0 + math.erf(x / math.sqrt(2.0)))
    Gy = [gelu(j * 0.25) - 0.5 * (j * 0.25) for j in range(17)]
    s = [Gy[j + 1] - Gy[j] for j in range(16)]
    return (256.0 * s[0],
            [256.0 * (s[j] - s[j - 1]) for j in range(1, 16)],
            32.0 - 256.0 * s[15],
            512.0 - 256.0 * Gy[16])


ENGS = ("sync", "gpsimd", "vector", "scalar", "tensor")


SEMKEYS = [e for e in ENGS] + [e + "d" for e in ENGS]
SELF_WAIT = {"vector", "scalar", "gpsimd"}


class Sched:
    def __init__(self):
        self.ops = {e: [] for e in ENGS}
        self.cnt = {k: 0 for k in SEMKEYS}

    def emit(self, eng, fn, waits=(), inc=1, key=None):
        key = key or eng
        before = self.cnt[key]
        self.ops[eng].append((fn, tuple(w for w in waits if w is not None), inc, key,
                              before))
        if inc:
            self.cnt[key] += inc
        return (key, self.cnt[key])

    def dma(self, eng, out, in_, waits=(), slow=False):
        if slow:
            return self.emit(eng, lambda e, o=out, i=in_: e.dma_start(
                out=o, in_=i, allow_slow_non_contiguous=True), waits=waits, inc=16,
                key=eng + "d")
        return self.emit(eng, lambda e, o=out, i=in_: e.dma_start(out=o, in_=i),
                         waits=waits, inc=16, key=eng + "d")

    def token(self, eng):
        return (eng, self.cnt[eng])

    def dtoken(self, eng):
        return (eng + "d", self.cnt[eng + "d"])

    def barrier(self):
        toks = [self.token(e) for e in ENGS] + [self.dtoken(e) for e in ENGS]
        for e in ENGS:
            waits = [t for t in toks if t[0] != e]
            self.emit(e, lambda eng: eng.nop(), waits=waits, inc=1)

    def replay(self, name, eng, sems):
        last = {}
        for fn, waits, inc, key, before in self.ops[name]:
            allw = list(waits)
            if name in SELF_WAIT and key == name and before > 0:
                allw.append((name, before))
            for (pkey, val) in allw:
                if last.get(pkey, 0) >= val:
                    continue
                eng.wait_ge(sems[pkey], val)
                last[pkey] = val
            ins = fn(eng)
            if inc:
                ins.then_inc(sems[key], inc)


def build_nc():
    c_lin, dels, a_e, b_e = _pwl_consts()
    nc = bass.Bass("TRN2", target_bir_lowering=False)

    x_own = nc.dram_tensor("x_own", [OWN, D], F32, kind="ExternalInput").ap()
    c_in = nc.dram_tensor("c_in", [1, D], F32, kind="ExternalInput").ap()
    wmod = nc.dram_tensor("wmod", [D, 6 * D], F32, kind="ExternalInput").ap()
    gamma = nc.dram_tensor("gamma", [D], F32, kind="ExternalInput").ap()
    beta = nc.dram_tensor("beta", [D], F32, kind="ExternalInput").ap()
    wqkv = nc.dram_tensor("wqkv", [D, 3 * D], F32, kind="ExternalInput").ap()
    wout = nc.dram_tensor("wout", [D, D], F32, kind="ExternalInput").ap()
    w1 = nc.dram_tensor("w1", [D, FF], F32, kind="ExternalInput").ap()
    b1 = nc.dram_tensor("b1", [FF], F32, kind="ExternalInput").ap()
    w2 = nc.dram_tensor("w2", [FF, D], F32, kind="ExternalInput").ap()
    b2 = nc.dram_tensor("b2", [D], F32, kind="ExternalInput").ap()
    out = nc.dram_tensor("out", [OWN, D], F32, kind="ExternalOutput").ap()
    if KDBG:
        dbg_mod = nc.dram_tensor("dbg_mod", [1, 6 * D], F32, kind="ExternalOutput").ap()
        dbg_bw = nc.dram_tensor("dbg_bw", [1, 3 * D], F32, kind="ExternalOutput").ap()
        dbg_kvks = nc.dram_tensor("dbg_kvks", [128, 1040], F32, kind="ExternalOutput").ap()
        dbg_hn = nc.dram_tensor("dbg_hn", [128, D], F32, kind="ExternalOutput").ap()
        dbg_q = nc.dram_tensor("dbg_q", [128, SUP], F32, kind="ExternalOutput").ap()
        dbg_att = nc.dram_tensor("dbg_att", [128, D], F32, kind="ExternalOutput").ap()
        dbg_x2 = nc.dram_tensor("dbg_x2", [128, D], F32, kind="ExternalOutput").ap()
        dbg_yq = nc.dram_tensor("dbg_yq", [128, SUP], F32, kind="ExternalOutput").ap()
    w1c = nc.dram_tensor("w1c", [FF // 128, 128, 1024], BF16).ap()
    w2c = nc.dram_tensor("w2c", [D // 128, 128, FF], BF16).ap()
    rowscr = nc.dram_tensor("rowscr", [1, D], F32).ap()
    cc_in = nc.dram_tensor("cc_in", [128, 1040], F32, kind="Internal",
                           addr_space="Local").ap() if False else \
        nc.dram_tensor("cc_in", [128, 1040], F32).ap()
    cc_out = nc.dram_tensor("cc_out", [128, 1040], F32).ap()

    S = Sched()

    with ExitStack() as st:
        e = st.enter_context
        ident = e(nc.sbuf_tensor("ident", [128, 128], BF16))
        wq16 = [e(nc.sbuf_tensor(f"wq16_{ch}", [128, D], BF16)) for ch in range(8)]
        wo16 = [e(nc.sbuf_tensor(f"wo16_{ch}", [128, D], BF16)) for ch in range(8)]
        bc_1ca = e(nc.sbuf_tensor("bc_1ca", [128, D], F32))
        bc_sa = e(nc.sbuf_tensor("bc_sa", [128, D], F32))
        bc_ga = e(nc.sbuf_tensor("bc_ga", [128, D], F32))
        bc_1cb = e(nc.sbuf_tensor("bc_1cb", [128, D], F32))
        bc_sb = e(nc.sbuf_tensor("bc_sb", [128, D], F32))
        bc_gb = e(nc.sbuf_tensor("bc_gb", [128, D], F32))
        bc_bw = e(nc.sbuf_tensor("bc_bw", [128, 2 * D], F32))
        bwq = e(nc.sbuf_tensor("bwq", [128, 8], F32))
        b1c = e(nc.sbuf_tensor("b1c", [128, 32], F32))
        b2c = e(nc.sbuf_tensor("b2c", [128, 8], F32))
        gfm = e(nc.sbuf_tensor("gfm", [128, 8], F32))
        kvks = e(nc.sbuf_tensor("kvks", [128, 16 * 65], BF16))
        scfm = e(nc.sbuf_tensor("scfm", [128, 8], F32))
        befm = e(nc.sbuf_tensor("befm", [128, 8], BF16))
        ioti = e(nc.sbuf_tensor("ioti", [128, 128], I32))

        PS_A = e(nc.psum_tensor("PS_A", [128, 2048], F32))
        PS_B = e(nc.psum_tensor("PS_B", [128, 1024], F32))
        PS_C = e(nc.psum_tensor("PS_C", [128, 512], F32))
        PS_T = e(nc.psum_tensor("PS_T", [128, 1024], BF16))

        if True:
            stA = ExitStack()
            eA = stA.enter_context
            wkv16 = [eA(nc.sbuf_tensor(f"wkv16_{ch}", [128, 2 * D], BF16)) for ch in range(8)]
            st0 = ExitStack()
            e0 = st0.enter_context
            wf = e0(nc.sbuf_tensor("wf", [128, 3 * D], F32))
            wmb = e0(nc.sbuf_tensor("wmb", [128, 8 * 512], F32))
            w1f = [e0(nc.sbuf_tensor(f"w1f{i}", [128, FF], BF16)) for i in range(2)]
            w2f = [e0(nc.sbuf_tensor(f"w2f{i}", [128, D], BF16)) for i in range(2)]
            cro = e0(nc.sbuf_tensor("cro", [128, 8], F32))
            modrow = e0(nc.sbuf_tensor("modrow", [1, 6 * D], F32))
            bwrow = e0(nc.sbuf_tensor("bwrow", [1, 3 * D], F32))
            sgnscr = e0(nc.sbuf_tensor("sgnscr", [1, D], F32))

            ti = S.emit("gpsimd", lambda g: g.iota(ioti[:], pattern=[[-1, 128]], base=0,
                                                   channel_multiplier=1))
            tid = S.emit("vector", lambda v: v.tensor_single_scalar(
                ident[:], ioti[:], 0.0, OP.is_equal), waits=[ti])


            tg = S.dma("sync", gfm[:], gamma.rearrange("(ch p) -> p ch", p=128), slow=True)
            tc = S.dma("sync", cro[:], c_in.rearrange("a (ch p) -> p (a ch)", p=128), slow=True)
            tbe = S.dma("gpsimd", befm[:], beta.rearrange("(ch p) -> p ch", p=128), slow=True)
            tb1 = S.dma("sync", b1c[:], b1.rearrange("(c p) -> p c", p=128), slow=True)
            tb2 = S.dma("sync", b2c[:], b2.rearrange("(c p) -> p c", p=128), slow=True)
            S.emit("vector", lambda v: v.tensor_scalar(b1c[:], b1c[:], 256.0, CRND + 0.5,
                                                       OP.mult, OP.add), waits=[tb1])
            S.emit("vector", lambda v: v.tensor_scalar(b2c[:], b2c[:], 256.0, CRND + 0.5,
                                                       OP.mult, OP.add), waits=[tb2])
            tsc = S.emit("scalar", lambda s: s.activation(scfm[:], cro[:], AF.Silu), waits=[tc])

            wun = e0(nc.sbuf_tensor("wun", [128, 3 * D], BF16))
            bw_ps = [PS_A[0:1, ct * 512:(ct + 1) * 512] for ct in range(4)] + \
                [PS_B[0:1, 0:512], PS_C[0:1, 0:512]]
            tprev = None
            for ch in range(8):
                td = S.dma("sync", wf[:], wqkv[ch * 128:(ch + 1) * 128, :],
                           waits=[tprev] if tprev else [])
                t1 = S.emit("vector", lambda v, ch=ch: v.tensor_scalar(
                    wq16[ch][:], wf[:, 0:D], gfm[:, ch:ch + 1], None, OP.mult),
                    waits=[td, tg])
                t2 = S.emit("vector", lambda v, ch=ch: v.tensor_scalar(
                    wkv16[ch][:], wf[:, D:3 * D], gfm[:, ch:ch + 1], None, OP.mult),
                    waits=[td])
                tun = S.emit("vector", lambda v: v.tensor_copy(wun[:], wf[:]), waits=[td])
                tprev = tun
                for ct in range(6):
                    S.emit("tensor", lambda t, ch=ch, ct=ct: (
                        t.ldweights(befm[:, ch:ch + 1]),
                        t.matmul(bw_ps[ct], befm[:, ch:ch + 1],
                                 wun[:, ct * 512:(ct + 1) * 512],
                                 start=(ch == 0), stop=(ch == 7)))[-1],
                        waits=([tun, tbe] if ct == 0 else ()), inc=1)
            wcast_done = S.token("vector")
            tbwmm = S.token("tensor")
            pev = None
            for ct in range(6):
                pev = S.emit("vector", lambda v, ct=ct: v.tensor_copy(
                    bwrow[0:1, ct * 512:(ct + 1) * 512], bw_ps[ct]), waits=[tbwmm])
            tbw = pev
            tbwq = S.dma("gpsimd", rowscr[0:1, :], bwrow[0:1, 0:D], waits=[tbw])
            tbwq = S.dma("gpsimd", bwq[:],
                         rowscr.rearrange("a (ch p) -> p (a ch)", p=128), waits=[tbwq],
                         slow=True)

            for ct in range(12):
                dts = None
                for ch in range(8):
                    dts = S.dma("sync", wmb[:, ch * 512:(ch + 1) * 512],
                                wmod[ch * 128:(ch + 1) * 128, ct * 512:(ct + 1) * 512],
                                waits=[pev] if ch == 0 else [])
                for ch in range(8):
                    S.emit("tensor", lambda t, ch=ch: t.matmul(
                        PS_C[0:1, :], scfm[:, ch:ch + 1],
                        wmb[:, ch * 512:(ch + 1) * 512],
                        start=(ch == 0), stop=(ch == 7)),
                        waits=([dts, tsc, pev] if ch == 0 else ()), inc=1)
                pev = S.emit("vector", lambda v, ct=ct: v.tensor_copy(
                    modrow[0:1, ct * 512:(ct + 1) * 512], PS_C[0:1, :]),
                    waits=[S.token("tensor")])
            tmod = pev

            S.emit("vector", lambda v: v.tensor_single_scalar(
                modrow[0:1, D:2 * D], modrow[0:1, D:2 * D], 1.0, OP.add), waits=[tmod])
            S.emit("vector", lambda v: v.tensor_single_scalar(
                modrow[0:1, 4 * D:5 * D], modrow[0:1, 4 * D:5 * D], 1.0, OP.add))
            for ofs in (2 * D, 5 * D):
                S.emit("vector", lambda v, o=ofs: v.tensor_single_scalar(
                    sgnscr[0:1, :], modrow[0:1, o:o + D], 0.0, OP.is_ge))
                S.emit("vector", lambda v: v.tensor_scalar(
                    sgnscr[0:1, :], sgnscr[0:1, :], 2.0, -1.0, OP.mult, OP.add))
                S.emit("scalar", lambda s_, o=ofs: s_.activation(
                    modrow[0:1, o:o + D], modrow[0:1, o:o + D], AF.Abs))
                S.emit("vector", lambda v, o=ofs: v.tensor_single_scalar(
                    modrow[0:1, o:o + D], modrow[0:1, o:o + D], 0.25, OP.max),
                    waits=[S.token("scalar")])
                S.emit("vector", lambda v, o=ofs: v.tensor_tensor(
                    modrow[0:1, o:o + D], modrow[0:1, o:o + D], sgnscr[0:1, :], OP.mult))
            tmod2 = S.token("vector")

            def bcast(dst, src_row, wtok, width):
                t = S.dma("sync", dst[0:1, 0:width], src_row, waits=[wtok])
                k = 1
                while k < 128:
                    t = S.dma("sync", dst[k:2 * k, 0:width], dst[0:k, 0:width], waits=[t])
                    k *= 2
                return t

            if KDBG:
                S.dma("sync", dbg_mod[0:1, :], modrow[0:1, :], waits=[tmod2])
                S.dma("sync", dbg_bw[0:1, :], bwrow[0:1, :], waits=[tbw])
            chains = [
                (bc_sa, modrow[0:1, 0:D], tmod2, D),
                (bc_1ca, modrow[0:1, D:2 * D], tmod2, D),
                (bc_ga, modrow[0:1, 2 * D:3 * D], tmod2, D),
                (bc_sb, modrow[0:1, 3 * D:4 * D], tmod2, D),
                (bc_1cb, modrow[0:1, 4 * D:5 * D], tmod2, D),
                (bc_gb, modrow[0:1, 5 * D:6 * D], tmod2, D),
                (bc_bw, bwrow[0:1, D:3 * D], tbw, 2 * D),
            ]
            toksb = [S.dma("sync", dst[0:1, 0:w], src, waits=[wt])
                     for dst, src, wt, w in chains]
            kk = 1
            while kk < 128:
                for i, (dst, src, wt, w) in enumerate(chains):
                    toksb[i] = S.dma("sync", dst[kk:2 * kk, 0:w], dst[0:kk, 0:w],
                                     waits=[toksb[i]])
                kk *= 2
            tb_sa, tb_ca, tb_ga, tb_sb, tb_cb, tb_gb, tb_bw = toksb

            for ch in range(8):
                S.dma("gpsimd", wo16[ch][:], wout[ch * 128:(ch + 1) * 128, :])
            two16 = S.token("gpsimd")


            outs_done = {}
            for ch in range(8):
                bi = ch % 2
                tin = S.dma("gpsimd", w1f[bi][:], w1[ch * 128:(ch + 1) * 128, :],
                            waits=[outs_done.get(bi)])
                t = None
                for grp in range(4):
                    src = w1f[bi][:, grp * 1024:(grp + 1) * 1024]
                    src = src.rearrange("p (c j) -> p c j", j=128)
                    dst = w1c[grp * 8:(grp + 1) * 8, :, ch * 128:(ch + 1) * 128]
                    t = S.dma("gpsimd", dst.transpose([1, 0, 2]), src, waits=[tin])
                outs_done[bi] = t
            w1c_done = S.token("gpsimd")
            for ch in range(FF // 128):
                bi = ch % 2
                tin = S.dma("gpsimd", w2f[bi][:], w2[ch * 128:(ch + 1) * 128, :],
                            waits=[outs_done.get(8 + bi)])
                src = w2f[bi][:].rearrange("p (oc j) -> p oc j", j=128)
                dst = w2c[:, :, ch * 128:(ch + 1) * 128]
                t = S.dma("gpsimd", dst.transpose([1, 0, 2]), src, waits=[tin])
                outs_done[8 + bi] = t
            w2c_done = S.token("gpsimd")

            S.barrier()
            st0.close()

            with ExitStack() as st1:
                e1 = st1.enter_context
                xt = [e1(nc.sbuf_tensor(f"xt{i}", [128, D], F32)) for i in range(2)]
                sdump = e1(nc.sbuf_tensor("sdump", [128, D], F32))
                xnb = e1(nc.sbuf_tensor("xnb", [128, D], F32))
                h1b = e1(nc.sbuf_tensor("h1b", [128, D], F32))
                hn16 = e1(nc.sbuf_tensor("hn16", [128, D], BF16))
                hnT = e1(nc.sbuf_tensor("hnT", [128, D], BF16))
                kb = e1(nc.sbuf_tensor("kb", [128, D], F32))
                mt = e1(nc.sbuf_tensor("mt", [128, D], F32))
                et = e1(nc.sbuf_tensor("et", [128, D], BF16))
                rt = e1(nc.sbuf_tensor("rt", [128, D], BF16))
                k16 = e1(nc.sbuf_tensor("k16", [128, D], BF16))
                vext = e1(nc.sbuf_tensor("vext", [128, 16 * 65], BF16))
                stq = e1(nc.sbuf_tensor("stq", [128, 8], F32))

                for hh in range(16):
                    S.emit("vector", lambda v, hh=hh: v.memset(
                        vext[:, hh * 65 + 64:hh * 65 + 65], 1.0))
                vext_ones = S.token("vector")

                stq_free = [None]

                def ln_stats(src, wtok, sqb, mu_sl, rstd_sl):
                    t1 = S.emit("scalar", lambda s: s.activation(
                        sdump[:], src, AF.Identity, accum_out=sqb[:, 0:1]),
                        waits=[wtok, stq_free[0]])
                    t2 = S.emit("scalar", lambda s: s.activation(
                        sdump[:], src, AF.Square, accum_out=sqb[:, 1:2]), waits=[wtok])
                    a1 = S.emit("vector", lambda v: v.tensor_single_scalar(
                        mu_sl, sqb[:, 0:1], 1.0 / D, OP.mult), waits=[t1])
                    a2 = S.emit("vector", lambda v: v.tensor_scalar(
                        sqb[:, 3:4], sqb[:, 1:2], 1.0 / D, LN_EPS, OP.mult, OP.add), waits=[t2])
                    stq_free[0] = a2
                    a3 = S.emit("vector", lambda v: v.tensor_tensor(
                        sqb[:, 4:5], mu_sl, mu_sl, OP.mult), waits=[a1])
                    a4 = S.emit("vector", lambda v: v.tensor_tensor(
                        sqb[:, 5:6], sqb[:, 3:4], sqb[:, 4:5], OP.subtract), waits=[a2, a3])
                    a5 = S.emit("scalar", lambda s: s.activation(
                        sqb[:, 6:7], sqb[:, 5:6], AF.Ln), waits=[a4])
                    return S.emit("scalar", lambda s: s.activation(
                        rstd_sl, sqb[:, 6:7], AF.Exp, scale=-0.5), waits=[a5])

                xprev = {}
                for i in range(NT_P1 if KPH >= 1 else 0):
                    xb = xt[i % 2]
                    tld = S.dma("sync", xb[:], x_own[i * 128:(i + 1) * 128, :],
                                waits=[xprev.get(i % 2)])
                    ts1 = ln_stats(xb[:], tld, stq, stq[:, 2:3], stq[:, 7:8])
                    t = S.emit("vector", lambda v, xb=xb: v.tensor_scalar(
                        xnb[:], xb[:], stq[:, 2:3], stq[:, 7:8], OP.subtract, OP.mult),
                        waits=[ts1])
                    xprev[i % 2] = t
                    t = S.emit("vector", lambda v: v.tensor_tensor(
                        h1b[:], xnb[:], bc_1ca[:], OP.mult),
                        waits=[t] + ([tb_ca, tb_sa] if i == 0 else []))
                    th = S.emit("vector", lambda v: v.tensor_tensor(
                        h1b[:], h1b[:], bc_sa[:], OP.add), waits=[t])
                    ts2 = ln_stats(h1b[:], th, stq, stq[:, 2:3], stq[:, 7:8])
                    thn = S.emit("vector", lambda v: v.tensor_scalar(
                        hn16[:], h1b[:], stq[:, 2:3], stq[:, 7:8], OP.subtract, OP.mult),
                        waits=[ts2])
                    for ch in range(8):
                        S.emit("tensor", lambda t, ch=ch: t.transpose(
                            PS_T[:, ch * 128:(ch + 1) * 128], hn16[:, ch * 128:(ch + 1) * 128],
                            ident[:]), waits=[thn, tid] if ch == 0 else ())
                    tT = S.token("tensor")
                    tev = S.emit("vector", lambda v: v.tensor_copy(hnT[:], PS_T[:]), waits=[tT])
                    for ch in range(8):
                        for cc in range(4):
                            S.emit("tensor", lambda t, ch=ch, cc=cc: (
                                t.ldweights(hnT[:, ch * 128:(ch + 1) * 128]) if cc == 0 else None,
                                t.matmul(PS_A[:, cc * 512:(cc + 1) * 512],
                                         hnT[:, ch * 128:(ch + 1) * 128],
                                         wkv16[ch][:, cc * 512:(cc + 1) * 512],
                                         start=(ch == 0), stop=(ch == 7)))[-1],
                                waits=([tev, wcast_done] if (ch == 0 and cc == 0) else ()),
                                inc=1)
                    tkv = S.token("tensor")
                    t = S.emit("vector", lambda v: v.tensor_tensor(
                        kb[:], PS_A[:, 0:D], bc_bw[:, 0:D], OP.add),
                        waits=[tkv] + ([tb_bw] if i == 0 else []))
                    t1 = S.emit("vector", lambda v: v.tensor_single_scalar(
                        mt[:], kb[:], 0.0, OP.min), waits=[t])
                    t2 = S.emit("scalar", lambda s: s.activation(et[:], mt[:], AF.Exp),
                                waits=[t1])
                    t3 = S.emit("vector", lambda v: v.tensor_single_scalar(
                        rt[:], kb[:], 0.0, OP.max), waits=[t])
                    tk = S.emit("vector", lambda v: v.tensor_tensor(
                        k16[:], et[:], rt[:], OP.add), waits=[t2, t3])
                    tv = S.emit("vector", lambda v: v.tensor_tensor(
                        vext[:].rearrange("p (h n) -> p h n", h=16)[:, :, 0:64],
                        PS_A[:, D:2 * D].rearrange("p (h n) -> p h n", h=16),
                        bc_bw[:, D:2 * D].rearrange("p (h n) -> p h n", h=16), OP.add),
                        waits=[tkv] + ([vext_ones] if i == 0 else []))
                    for pr in range(8):
                        ps = PS_B[:, 0:260] if pr < 4 else PS_B[:, 512:772]
                        co = 65 * (pr % 4)
                        h0, h1_ = 2 * pr, 2 * pr + 1
                        S.emit("tensor", lambda t, ps=ps, co=co, h0=h0, i=i: (
                            t.ldweights(k16[:, h0 * 64:(h0 + 1) * 64], tile_position=(0, 0)),
                            t.matmul(ps[0:64, co:co + 65], k16[:, h0 * 64:(h0 + 1) * 64],
                                     vext[:, h0 * 65:(h0 + 1) * 65],
                                     start=(i == 0), stop=(i == NT_P1 - 1),
                                     tile_position=(0, 0)))[-1],
                            waits=([tk, tv] if pr == 0 else ()), inc=1)
                        S.emit("tensor", lambda t, ps=ps, co=co, h1_=h1_, i=i: (
                            t.ldweights(k16[:, h1_ * 64:(h1_ + 1) * 64], tile_position=(0, 64)),
                            t.matmul(ps[64:128, co:co + 65], k16[:, h1_ * 64:(h1_ + 1) * 64],
                                     vext[:, h1_ * 65:(h1_ + 1) * 65],
                                     start=(i == 0), stop=(i == NT_P1 - 1),
                                     tile_position=(0, 64)))[-1], inc=1)
                kv_mm_tok = S.token("tensor")
                kvf32 = e1(nc.sbuf_tensor("kvf32", [128, 1040], F32))
                if KPH >= 1:
                    zf = S.emit("vector", lambda v: v.memset(kvf32[:], 0.0))
                    for pr in range(8):
                        ps = PS_B[:, 0:260] if pr < 4 else PS_B[:, 512:772]
                        co = 65 * (pr % 4)
                        h0, h1_ = 2 * pr, 2 * pr + 1
                        S.emit("vector", lambda v, ps=ps, co=co, h0=h0: v.tensor_copy(
                            kvf32[0:64, h0 * 65:(h0 + 1) * 65], ps[0:64, co:co + 65]),
                            waits=[kv_mm_tok, zf])
                        S.emit("vector", lambda v, ps=ps, co=co, h1_=h1_: v.tensor_copy(
                            kvf32[64:128, h1_ * 65:(h1_ + 1) * 65], ps[64:128, co:co + 65]),
                            waits=[kv_mm_tok])
                    tkv_ev = S.token("vector")
                    tcc1 = S.dma("sync", cc_in[:, :], kvf32[:], waits=[tkv_ev])
                    tcol = S.emit("gpsimd", lambda g: g.collective_compute(
                        "AllReduce", OP.add,
                        replica_groups=[[0, 1, 2, 3], [4, 5, 6, 7]],
                        ins=[cc_in[:, :]], outs=[cc_out[:, :]]), waits=[tcc1])
                    tcc2 = S.dma("gpsimd", kvf32[:], cc_out[:, :], waits=[tcol])
                    S.emit("vector", lambda v: v.tensor_copy(kvks[:], kvf32[:]),
                           waits=[tcc2])

                if KDBG:
                    S.dma("gpsimd", dbg_kvks[:, :], kvks[:], waits=[S.token("vector")])
                S.barrier()

            stA.close()
            with ExitStack() as st2:
                e2 = st2.enter_context
                xt2 = [e2(nc.sbuf_tensor("xt2_0", [128, D], F32))]
                sdump2 = e2(nc.sbuf_tensor("sdump2", [128, D], BF16))
                hb2 = e2(nc.sbuf_tensor("hb2", [128, D], F32))
                hn2 = e2(nc.sbuf_tensor("hn2", [128, D], BF16))
                hnTs = e2(nc.sbuf_tensor("hnTs", [128, 8 * SUP], BF16))
                h2Ts = e2(nc.sbuf_tensor("h2Ts", [128, 8 * SUP], BF16))
                q16 = [e2(nc.sbuf_tensor(f"q16_{qc}", [128, SUP], BF16)) for qc in range(8)]
                mt2 = e2(nc.sbuf_tensor("mt2", [128, SUP], F32))
                et2 = e2(nc.sbuf_tensor("et2", [128, SUP], BF16))
                rt2 = e2(nc.sbuf_tensor("rt2", [128, SUP], BF16))
                att = e2(nc.sbuf_tensor("att", [128, D], BF16))
                attT = e2(nc.sbuf_tensor("attT", [128, D], BF16))
                oq16s = e2(nc.sbuf_tensor("oq16s", [128, 8 * SUP], BF16))
                rcd = e2(nc.sbuf_tensor("rcd", [128, 32], F32))
                x2s = e2(nc.sbuf_tensor("x2s", [128, 4 * D], BF16))
                st1v = e2(nc.sbuf_tensor("st1v", [128, 8], F32))
                stq2 = e2(nc.sbuf_tensor("stq2", [128, 8], F32))
                yq = e2(nc.sbuf_tensor("yq", [128, 32 * SUP], BF16))
                w1s = [e2(nc.sbuf_tensor(f"w1s_{i}", [128, 1024], BF16)) for i in range(2)]
                w2s = e2(nc.sbuf_tensor("w2s", [128, FF], BF16))
                W2 = 2 * SUP
                ib = [e2(nc.sbuf_tensor(f"ib{i}", [128, W2], I32)) for i in range(2)]
                fbf = [e2(nc.sbuf_tensor(f"fbf{i}", [128, W2], F32)) for i in range(2)]
                tbf = [e2(nc.sbuf_tensor(f"tbf{i}", [128, W2], BF16)) for i in range(2)]
                ubf = e2(nc.sbuf_tensor("ubf", [128, W2], BF16))
                acc = [e2(nc.sbuf_tensor(f"acc{i}", [128, W2], BF16)) for i in range(2)]
                abuf = [e2(nc.sbuf_tensor(f"abuf{i}", [128, W2], BF16)) for i in range(3)]
                btab = e2(nc.sbuf_tensor("btab", [128, 17], F32))
                x3b = [e2(nc.sbuf_tensor("x3b0", [128, D], F32))]

                stq2_free = [None]

                def ln_stats2(src, wtok, mu_sl, rstd_sl):
                    t1 = S.emit("scalar", lambda s: s.activation(
                        sdump2[:], src, AF.Identity, accum_out=stq2[:, 0:1]),
                        waits=[wtok, stq2_free[0]])
                    t2 = S.emit("scalar", lambda s: s.activation(
                        sdump2[:], src, AF.Square, accum_out=stq2[:, 1:2]), waits=[wtok])
                    a1 = S.emit("vector", lambda v: v.tensor_single_scalar(
                        mu_sl, stq2[:, 0:1], 1.0 / D, OP.mult), waits=[t1])
                    a2 = S.emit("vector", lambda v: v.tensor_scalar(
                        stq2[:, 3:4], stq2[:, 1:2], 1.0 / D, LN_EPS, OP.mult, OP.add),
                        waits=[t2])
                    stq2_free[0] = a2
                    a3 = S.emit("vector", lambda v: v.tensor_tensor(
                        stq2[:, 4:5], mu_sl, mu_sl, OP.mult), waits=[a1])
                    a4 = S.emit("vector", lambda v: v.tensor_tensor(
                        stq2[:, 5:6], stq2[:, 3:4], stq2[:, 4:5], OP.subtract), waits=[a2, a3])
                    a5 = S.emit("scalar", lambda s: s.activation(
                        stq2[:, 6:7], stq2[:, 5:6], AF.Ln), waits=[a4])
                    return S.emit("scalar", lambda s: s.activation(
                        rstd_sl, stq2[:, 6:7], AF.Exp, scale=-0.5), waits=[a5])

                _coefs = [c_lin] + dels + [a_e]
                for j in range(17):
                    S.emit("vector", lambda v, j=j: v.memset(
                        btab[:, j:j + 1], -abs(_coefs[j]) * float(j)))
                btab_done = S.token("vector")

                xprev = {}
                psA_free = [None, None]
                qmm_done = None
                w1mm_done = None
                w2mm_done = None
                oqT_done = None
                w2s_free = None
                zfree = {}
                qread = None
                for sp in range(NSUP if KPH >= 2 else 0):
                    for t4 in range(4):
                        gi = sp * 4 + t4
                        xb = xt2[0]
                        tld = S.dma("sync", xb[:], x_own[gi * 128:(gi + 1) * 128, :],
                                    waits=[xprev.get(0)])
                        tsA = ln_stats2(xb[:], tld, st1v[:, 2 * t4:2 * t4 + 1],
                                        st1v[:, 2 * t4 + 1:2 * t4 + 2])
                        t = S.emit("vector", lambda v, xb=xb, t4=t4: v.tensor_scalar(
                            hb2[:], xb[:], st1v[:, 2 * t4:2 * t4 + 1],
                            st1v[:, 2 * t4 + 1:2 * t4 + 2], OP.subtract, OP.mult),
                            waits=[tsA])
                        xprev[0] = t
                        t = S.emit("vector", lambda v: v.tensor_tensor(
                            hb2[:], hb2[:], bc_1ca[:], OP.mult),
                            waits=[t] + ([tb_ca, tb_sa, tb_ga, tb_sb, tb_cb, tb_gb]
                                         if sp == 0 and t4 == 0 else []))
                        th = S.emit("vector", lambda v: v.tensor_tensor(
                            hb2[:], hb2[:], bc_sa[:], OP.add), waits=[t])
                        tsB = ln_stats2(hb2[:], th, stq2[:, 2:3], stq2[:, 7:8])
                        thn = S.emit("vector", lambda v: v.tensor_scalar(
                            hn2[:], hb2[:], stq2[:, 2:3], stq2[:, 7:8],
                            OP.subtract, OP.mult), waits=[tsB])
                        if KDBG and sp == 0 and t4 == 0:
                            S.dma("gpsimd", dbg_hn[:, :], hn2[:], waits=[thn])
                            S.barrier()
                        for ch in range(8):
                            S.emit("tensor", lambda t, ch=ch: t.transpose(
                                PS_T[:, ch * 128:(ch + 1) * 128],
                                hn2[:, ch * 128:(ch + 1) * 128], ident[:]),
                                waits=[thn] if ch == 0 else ())
                        tT = S.token("tensor")
                        S.emit("vector", lambda v, t4=t4: v.tensor_copy(
                            hnTs[:].rearrange("p (ch n) -> p ch n", ch=8)
                            [:, :, t4 * 128:(t4 + 1) * 128],
                            PS_T[:].rearrange("p (ch n) -> p ch n", ch=8)),
                            waits=[tT] + ([qmm_done] if t4 == 0 else []))
                    thnT = S.token("vector")

                    for qc in range(8 if KP2 >= 2 else 0):
                        for ch in range(8):
                            S.emit("tensor", lambda t, qc=qc, ch=ch: (
                                t.ldweights(wq16[ch][:, qc * 128:(qc + 1) * 128]),
                                t.matmul(PS_C[:, :], wq16[ch][:, qc * 128:(qc + 1) * 128],
                                         hnTs[:, ch * SUP:(ch + 1) * SUP],
                                         start=(ch == 0), stop=(ch == 7)))[-1],
                                waits=([thnT, qread] if ch == 0 else ()), inc=1)
                        tq = S.token("tensor")
                        t1 = S.emit("vector", lambda v, qc=qc: v.tensor_scalar(
                            mt2[:], PS_C[:, :], bwq[:, qc:qc + 1], 0.0, OP.add, OP.min),
                            waits=[tq, tbwq])
                        t2 = S.emit("scalar", lambda s: s.activation(
                            et2[:], mt2[:], AF.Exp), waits=[t1])
                        qread = S.emit("vector", lambda v, qc=qc: v.tensor_scalar(
                            rt2[:], PS_C[:, :], bwq[:, qc:qc + 1], 0.0, OP.add, OP.max),
                            waits=[tq])
                        tq16 = S.emit("vector", lambda v, qc=qc: v.tensor_tensor(
                            q16[qc][:], et2[:], rt2[:], OP.add), waits=[t2])
                        if KDBG and sp == 0 and qc == 0:
                            S.dma("gpsimd", dbg_q[:, :], q16[0][:], waits=[tq16])
                            S.barrier()
                    tqall = S.token("vector")

                    for t4 in range(4 if KP2 >= 3 else 0):
                        gi = sp * 4 + t4
                        xb = xt2[0]
                        tld = S.dma("sync", xb[:], x_own[gi * 128:(gi + 1) * 128, :],
                                    waits=[xprev.get(0)])
                        for nh, hh in enumerate(list(range(0, 16, 2)) + list(range(1, 16, 2))):
                            qc, ro = hh // 2, (hh % 2) * 64
                            S.emit("tensor", lambda t, qc=qc, ro=ro, hh=hh, t4=t4: (
                                t.ldweights(q16[qc][ro:ro + 64, t4 * 128:(t4 + 1) * 128],
                                            tile_position=(ro, 0)),
                                t.matmul(PS_A[:, hh * 128:hh * 128 + 65],
                                         q16[qc][ro:ro + 64, t4 * 128:(t4 + 1) * 128],
                                         kvks[ro:ro + 64, hh * 65:(hh + 1) * 65],
                                         start=True, stop=True, tile_position=(ro, 0)))[-1],
                                waits=([tqall] + psA_free if nh == 0 and t4 == 0
                                       else ([tqall] if nh == 0 else ())), inc=1)
                        tat = S.token("tensor")
                        t1 = S.emit("vector", lambda v: v.tensor_single_scalar(
                            rcd[:, 0:16],
                            PS_A[:].rearrange("p (h n) -> p h n", h=16)[:, :, 64:65],
                            DEN_EPS, OP.add), waits=[tat])
                        t2 = S.emit("vector", lambda v: v.reciprocal(
                            rcd[:, 16:32], rcd[:, 0:16]), waits=[t1])
                        tat2 = S.emit("vector", lambda v: v.tensor_tensor(
                            att[:].rearrange("p (h n) -> p h n", h=16),
                            PS_A[:].rearrange("p (h n) -> p h n", h=16)[:, :, 0:64],
                            rcd[:, 16:32].unsqueeze(2).broadcast_to([128, 16, 64]), OP.mult),
                            waits=[t2])
                        if KP3 < 2:
                            continue
                        if KDBG and sp == 0 and t4 == 0:
                            S.dma("gpsimd", dbg_att[:, :], att[:], waits=[tat2])
                            S.barrier()
                        for ch in range(8):
                            S.emit("tensor", lambda t, ch=ch: t.transpose(
                                PS_T[:, ch * 128:(ch + 1) * 128],
                                att[:, ch * 128:(ch + 1) * 128], ident[:]),
                                waits=[tat2] if ch == 0 else ())
                        tT = S.token("tensor")
                        tev = S.emit("vector", lambda v: v.tensor_copy(attT[:], PS_T[:]),
                                     waits=[tT])
                        for ch in range(8):
                            for cc in range(2):
                                S.emit("tensor", lambda t, ch=ch, cc=cc: (
                                    t.ldweights(attT[:, ch * 128:(ch + 1) * 128])
                                    if cc == 0 else None,
                                    t.matmul(PS_B[:, cc * 512:(cc + 1) * 512],
                                             attT[:, ch * 128:(ch + 1) * 128],
                                             wo16[ch][:, cc * 512:(cc + 1) * 512],
                                             start=(ch == 0), stop=(ch == 7)))[-1],
                                    waits=([tev, two16] if (ch == 0 and cc == 0) else ()),
                                    inc=1)
                        two = S.token("tensor")
                        if KP3 < 3:
                            continue
                        t = S.emit("vector", lambda v, xb=xb, t4=t4: v.tensor_scalar(
                            hb2[:], xb[:], st1v[:, 2 * t4:2 * t4 + 1],
                            st1v[:, 2 * t4 + 1:2 * t4 + 2], OP.subtract, OP.mult),
                            waits=[tld])
                        t = S.emit("vector", lambda v: v.tensor_tensor(
                            hb2[:], hb2[:], bc_1ca[:], OP.mult), waits=[t])
                        t = S.emit("vector", lambda v: v.tensor_tensor(
                            hb2[:], hb2[:], bc_sa[:], OP.add), waits=[t])
                        t = S.emit("vector", lambda v: v.tensor_tensor(
                            hb2[:], hb2[:], PS_B[:], OP.add), waits=[t, two])
                        t = S.emit("vector", lambda v: v.tensor_tensor(
                            hb2[:], hb2[:], bc_ga[:], OP.mult), waits=[t])
                        tx2 = S.emit("vector", lambda v, xb=xb, t4=t4: v.tensor_tensor(
                            x2s[:, t4 * D:(t4 + 1) * D], hb2[:], xb[:], OP.add), waits=[t])
                        xprev[0] = tx2
                        if KDBG and sp == 0 and t4 == 0:
                            S.dma("sync", dbg_x2[:, :], x2s[:, 0:D], waits=[tx2])
                            S.barrier()
                        if KP3 < 4:
                            continue
                        tsB = ln_stats2(x2s[:, t4 * D:(t4 + 1) * D], tx2,
                                        stq2[:, 2:3], stq2[:, 7:8])
                        t = S.emit("vector", lambda v, t4=t4: v.tensor_scalar(
                            hb2[:], x2s[:, t4 * D:(t4 + 1) * D], stq2[:, 2:3], stq2[:, 7:8],
                            OP.subtract, OP.mult), waits=[tsB])
                        t = S.emit("vector", lambda v: v.tensor_tensor(
                            hb2[:], hb2[:], bc_1cb[:], OP.mult), waits=[t])
                        th2 = S.emit("vector", lambda v: v.tensor_tensor(
                            hn2[:], hb2[:], bc_sb[:], OP.add), waits=[t])
                        for ch in range(8):
                            S.emit("tensor", lambda t, ch=ch: t.transpose(
                                PS_T[:, ch * 128:(ch + 1) * 128],
                                hn2[:, ch * 128:(ch + 1) * 128], ident[:]),
                                waits=[th2] if ch == 0 else ())
                        tT = S.token("tensor")
                        S.emit("vector", lambda v, t4=t4: v.tensor_copy(
                            h2Ts[:].rearrange("p (ch n) -> p ch n", ch=8)
                            [:, :, t4 * 128:(t4 + 1) * 128],
                            PS_T[:].rearrange("p (ch n) -> p ch n", ch=8)),
                            waits=[tT] + ([w1mm_done] if t4 == 0 else []))
                    th2T = S.token("vector")

                    coefs = [c_lin] + dels + [a_e]
                    w1buf_free = {}
                    ring = {}
                    tbf_free = {}
                    preps = {}
                    yq_tok = {}

                    def emit_prep(p, sp):
                        c0 = 2 * p
                        pp = p % 2
                        nonlocal_toks = {}
                        for c in (c0, c0 + 1):
                            bi = c % 2
                            tw = S.dma("sync", w1s[bi][:], w1c[c, :, :],
                                       waits=[w1buf_free.get(bi)] + ([w1c_done] if sp == 0 and c < 2 else []))
                            zps = PS_C[:, :] if bi == 0 else PS_B[:, 0:512]
                            for ch in range(8):
                                S.emit("tensor", lambda t, bi=bi, ch=ch, zps=zps: (
                                    t.ldweights(w1s[bi][:, ch * 128:(ch + 1) * 128]),
                                    t.matmul(zps, w1s[bi][:, ch * 128:(ch + 1) * 128],
                                             h2Ts[:, ch * SUP:(ch + 1) * SUP],
                                             start=(ch == 0), stop=(ch == 7)))[-1],
                                    waits=([tw, th2T, zfree.get(bi), qread]
                                           if ch == 0 else ()), inc=1)
                            w1buf_free[bi] = S.token("tensor")
                            tz = S.token("tensor")
                            t = S.emit("vector", lambda v, zps=zps, c=c, bi=bi, pp=pp:
                                       v.tensor_scalar(
                                ib[pp][:, bi * SUP:(bi + 1) * SUP], zps, 256.0,
                                b1c[:, c:c + 1], OP.mult, OP.add),
                                waits=[tz] + ([w2mm_done] if c == 0 else []))
                            zfree[bi] = t
                        tf = S.emit("vector", lambda v, pp=pp: v.tensor_single_scalar(
                            fbf[pp][:], ib[pp][:], -CRND, OP.add), waits=[t])
                        ttb = S.emit("scalar", lambda s_, pp=pp: s_.activation(
                            tbf[pp][:], fbf[pp][:], AF.Abs, scale=1.0 / 64.0),
                            waits=[tf, tbf_free.get(pp)])
                        preps[p] = (tf, ttb)

                    def emit_chain(p, sp):
                        c0 = 2 * p
                        pp = p % 2
                        tf, ttb = preps.pop(p)
                        ta = S.emit("vector", lambda v, pp=pp: v.tensor_single_scalar(
                            acc[0][:], fbf[pp][:], 0.5, OP.mult), waits=[tf])
                        pa = 0
                        tr = None
                        for j in range(17):
                            co = coefs[j]
                            if j in ():
                                tu2 = S.emit("vector", lambda v, j=j, pp=pp: v.tensor_scalar(
                                    ubf[:], tbf[pp][:], -float(j), 0.0, OP.add, OP.max),
                                    waits=[ttb, ta])
                                ta = S.emit("vector", lambda v, pa=pa, co=co:
                                            v.scalar_tensor_tensor(
                                    acc[1 - pa][:], ubf[:], co, acc[pa][:],
                                    OP.mult, OP.add), waits=[tu2])
                                pa = 1 - pa
                                continue
                            ab = abuf[j % 3]
                            tr = S.emit("scalar", lambda s_, j=j, co=co, ab=ab, pp=pp:
                                        s_.activation(
                                ab[:], tbf[pp][:], AF.Relu, scale=abs(co),
                                bias=btab[:, j:j + 1]),
                                waits=[ttb, ring.get(j % 3), btab_done])
                            ta = S.emit("vector", lambda v, pa=pa, ab=ab, co=co:
                                        v.tensor_tensor(
                                acc[1 - pa][:], acc[pa][:], ab[:],
                                OP.add if co > 0 else OP.subtract), waits=[tr, ta])
                            ring[j % 3] = ta
                            pa = 1 - pa
                        tbf_free[pp] = tr
                        tu = S.emit("vector", lambda v, pp=pp: v.tensor_single_scalar(
                            ubf[:], tbf[pp][:], 16.0, OP.is_gt), waits=[ttb])
                        ta = S.emit("vector", lambda v, pa=pa: v.scalar_tensor_tensor(
                            acc[1 - pa][:], ubf[:], b_e, acc[pa][:], OP.mult, OP.add),
                            waits=[tu, ta])
                        pa = 1 - pa
                        t = S.emit("vector", lambda v, pa=pa, pp=pp: v.tensor_single_scalar(
                            ib[pp][:], acc[pa][:], CRND + 0.5, OP.add), waits=[ta])
                        yq_tok[p] = S.emit("vector", lambda v, c0=c0, pp=pp: v.tensor_scalar(
                            yq[:, c0 * SUP:(c0 + 2) * SUP], ib[pp][:], -CRND, 1.0 / 256.0,
                            OP.add, OP.mult), waits=[t])

                    NPAIR = 16 if KP2 >= 4 else 0
                    for p in range(NPAIR):
                        emit_prep(p, sp)
                        if p > 0:
                            emit_chain(p - 1, sp)
                    if NPAIR:
                        emit_chain(NPAIR - 1, sp)
                    tyq = S.token("vector")

                    w2read = {}
                    for oc in range(8 if KP2 >= 5 else 0):
                        bi = oc % 2
                        zps = PS_A[:, bi * 512:(bi + 1) * 512]
                        tw = S.dma("sync", w2s[:], w2c[oc, :, :],
                                   waits=[w2s_free] + ([w2c_done] if sp == 0 and oc == 0 else []))
                        for ch in range(32):
                            S.emit("tensor", lambda t, ch=ch, zps=zps: (
                                t.ldweights(w2s[:, ch * 128:(ch + 1) * 128]),
                                t.matmul(zps, w2s[:, ch * 128:(ch + 1) * 128],
                                         yq[:, ch * SUP:(ch + 1) * SUP],
                                         start=(ch == 0), stop=(ch == 31)))[-1],
                                waits=[yq_tok.get(ch // 2)] +
                                      ([tw, w2read.get(bi)] if ch == 0 else []), inc=1)
                        w2s_free = S.token("tensor")
                        tz = S.token("tensor")
                        t = S.emit("vector", lambda v, zps=zps, oc=oc: v.tensor_scalar(
                            ib[0][:, 0:SUP], zps, 256.0, b2c[:, oc:oc + 1], OP.mult, OP.add),
                            waits=[tz] + ([oqT_done] if oc == 0 else []))
                        w2read[bi] = t
                        S.emit("vector", lambda v, oc=oc: v.tensor_scalar(
                            oq16s[:, oc * SUP:(oc + 1) * SUP], ib[0][:, 0:SUP], -CRND,
                            1.0 / 256.0, OP.add, OP.mult), waits=[t])
                    w2mm_done = w2s_free
                    psA_free = [w2read.get(0), w2read.get(1)]
                    qread = None
                    zfree = {0: zfree.get(0), 1: zfree.get(1)}
                    toqa = S.token("vector")

                    oqf_done = None
                    st_done = {}
                    for t4 in range(4 if KP2 >= 9 else 0):
                        for ch in range(8):
                            S.emit("tensor", lambda t, ch=ch, t4=t4: t.transpose(
                                PS_T[:, ch * 128:(ch + 1) * 128],
                                oq16s[:, ch * SUP + t4 * 128: ch * SUP + (t4 + 1) * 128],
                                ident[:]),
                                waits=([toqa, oqf_done] if ch == 0 else ()))
                        tT = S.token("tensor")
                        x3 = x3b[0]
                        oqf_done = S.emit("vector", lambda v, x3=x3: v.tensor_copy(
                            x3[:], PS_T[:]), waits=[tT, st_done.get(0)])
                        t = S.emit("vector", lambda v, x3=x3: v.tensor_tensor(
                            x3[:], x3[:], bc_gb[:], OP.mult), waits=[oqf_done])
                        tcv = S.emit("vector", lambda v, t4=t4: v.tensor_copy(
                            hb2[:], x2s[:, t4 * D:(t4 + 1) * D]), waits=[t])
                        tx3 = S.emit("vector", lambda v, x3=x3: v.tensor_tensor(
                            x3[:], x3[:], hb2[:], OP.add), waits=[tcv])
                        gi = sp * 4 + t4
                        st_done[0] = S.dma("sync", out[gi * 128:(gi + 1) * 128, :], x3[:],
                                           waits=[tx3])
                    oqT_done = S.token("tensor")
                    qmm_done = w2s_free
                    w1mm_done = w2s_free

            if KPH < 2 or KP2 < 9:
                with ExitStack() as std:
                    zb = std.enter_context(nc.sbuf_tensor("zb", [128, D], F32))
                    tz0 = S.emit("vector", lambda v: v.memset(zb[:], 0.0))
                    for gi in range(OWN // 128):
                        S.dma("sync", out[gi * 128:(gi + 1) * 128, :], zb[:], waits=[tz0])

        with ExitStack() as stb:
            eb = stb.enter_context
            sems = {k: eb(nc.semaphore(f"sem_{k}")) for k in SEMKEYS}
            block = eb(nc.Block())

            @block.sync
            def _(eng):
                S.replay("sync", eng, sems)
                eng.wait_ge(sems["syncd"], S.cnt["syncd"])

            @block.gpsimd
            def _(eng):
                S.replay("gpsimd", eng, sems)

            @block.vector
            def _(eng):
                S.replay("vector", eng, sems)

            @block.scalar
            def _(eng):
                S.replay("scalar", eng, sems)

            @block.tensor
            def _(eng):
                S.replay("tensor", eng, sems)

    return nc


_NC_CACHE = None


def kernel(L0_attn_x, L0_attn_gamma, L0_attn_beta, L0_attn_Wqkv, L0_attn_Wout,
           L0_ffn_w1, L0_ffn_b1, L0_ffn_w2, L0_ffn_b2, L0_c, L0_wmod):
    global _NC_CACHE
    x = np.ascontiguousarray(np.asarray(L0_attn_x, dtype=np.float32))
    if _NC_CACHE is None:
        _NC_CACHE = build_nc()
    nc = _NC_CACHE
    in_maps = []
    for core in range(8):
        b, q = core // 4, core % 4
        in_maps.append({
            "x_own": x[b, q * OWN:(q + 1) * OWN],
            "c_in": np.ascontiguousarray(np.asarray(L0_c, np.float32)[b:b + 1]),
            "wmod": np.asarray(L0_wmod, np.float32),
            "gamma": np.asarray(L0_attn_gamma, np.float32),
            "beta": np.asarray(L0_attn_beta, np.float32),
            "wqkv": np.asarray(L0_attn_Wqkv, np.float32),
            "wout": np.asarray(L0_attn_Wout, np.float32),
            "w1": np.asarray(L0_ffn_w1, np.float32),
            "b1": np.asarray(L0_ffn_b1, np.float32),
            "w2": np.asarray(L0_ffn_w2, np.float32),
            "b2": np.asarray(L0_ffn_b2, np.float32),
        })
    res = run_bass_kernel_spmd(nc, in_maps, core_ids=list(range(8)))
    outp = np.empty((2, LB, D), dtype=np.float32)
    for core in range(8):
        b, q = core // 4, core % 4
        outp[b, q * OWN:(q + 1) * OWN] = res.results[core]["out"]
    return outp
